# version 08
# baseline (speedup 1.0000x reference)
import numpy as np
import ml_dtypes
import concourse.bass as bass
import concourse.mybir as mybir
from concourse.bass_utils import run_bass_kernel_spmd

F32 = mybir.dt.float32
BF16 = mybir.dt.bfloat16
AF = mybir.ActivationFunctionType
ALU = mybir.AluOpType
AX = mybir.AxisListType

T = 2048
D = 1024
DI = 2048
DFF = 4096
NE = 8
NH = 8
DST = 16
DTR = 64
KD = D // 128
KI = DI // 128
NDFF = DFF // 128
TC = 512
NCH = T // TC
NTT = T // 128
LN_EPS = 1e-5
ATT_SCALE = 16.0
ATT_EPS = 1e-6
L2_EPS = 1e-12
KB = 1024
DBG_FFN_CHUNKS = None
DBG_FFN_PART = None


class Sem:
    def __init__(self, nc, name):
        self.h = nc.alloc_semaphore(name)
        self.n = 0


class Rot:
    def __init__(self, n):
        self.last = [None] * n

    def wait(self, Wfn, eng, j):
        Wfn(eng, self.last[j])

    def set(self, j, mark):
        self.last[j] = mark


def build(stage="full"):
    nc = bass.Bass("TRN2", target_bir_lowering=False, debug=False)

    def hin(name, shape, dtype=F32):
        return nc.dram_tensor(name, shape, dtype, kind="ExternalInput")

    xT_h = hin("xT", [D, T])
    wq_h = hin("wq", [D, 512])
    wkv_h = hin("wkv", [D, 1024])
    wo_h = hin("wo", [512, D])
    wr_h = hin("wr", [D, NE])
    brc_h = hin("brc", [NE, 1])
    w1_h = hin("w1", [NE, D, DFF])
    w2_h = hin("w2", [NE, D, DFF])
    w3_h = hin("w3", [NE, DFF, D])
    b1_h = hin("b1", [128, NE, NDFF])
    b2_h = hin("b2", [128, NE, NDFF])
    b3_h = hin("b3", [128, NE, KD])
    winT_h = hin("winT", [D, 2 * DI])
    wxT_h = hin("wxT", [DI, DTR + 2 * DST])
    wdtT_h = hin("wdtT", [DTR, DI])
    woutT_h = hin("woutT", [DI, D])
    convw_h = hin("convw", [128, KI, 4])
    convb_h = hin("convb", [128, KI])
    bdt_h = hin("bdt", [128, KI])
    alog_h = hin("alog", [128, KI, DST])
    dskip_h = hin("dskip", [128, KI])
    ident_h = hin("identc", [128, 128])
    ones_h = hin("onesc", [128, 128])
    sel2_h = hin("sel2c", [2, 128])
    blk2_h = hin("blk2c", [128, 2])
    sel8_h = hin("sel8c", [NE, NE * 128])
    selb_h = hin("selbc", [DST, DST * 128])

    outT_h = nc.dram_tensor("outT", [D, T], F32, kind="ExternalOutput")

    w1s = nc.dram_tensor("w1s", [NE, D, DFF], BF16, kind="Internal")
    w2s = nc.dram_tensor("w2s", [NE, D, DFF], BF16, kind="Internal")
    w3s = nc.dram_tensor("w3s", [NE, DFF, D], BF16, kind="Internal")
    winTs = nc.dram_tensor("winTs", [D, 2 * DI], BF16, kind="Internal")
    woutTs = nc.dram_tensor("woutTs", [DI, D], BF16, kind="Internal")
    wxTs = nc.dram_tensor("wxTs", [DI, DTR + 2 * DST], BF16, kind="Internal")
    wdtTs = nc.dram_tensor("wdtTs", [DTR, DI], BF16, kind="Internal")

    xT = nc.alloc_sbuf_tensor("xTs", [128, KD, T], F32)
    ident = nc.alloc_sbuf_tensor("ident", [128, 128], F32)
    ones = nc.alloc_sbuf_tensor("ones", [128, 128], F32)
    sel2 = nc.alloc_sbuf_tensor("sel2", [2, 128], F32)
    blk2 = nc.alloc_sbuf_tensor("blk2", [128, 2], F32)
    sel8 = nc.alloc_sbuf_tensor("sel8", [NE, NE * 128], F32)
    A_sb = nc.alloc_sbuf_tensor("A_sb", [128, KI, DST], F32)
    convw = nc.alloc_sbuf_tensor("convws", [128, KI, 4], F32)
    convb = nc.alloc_sbuf_tensor("convbs", [128, KI], F32)
    bdt = nc.alloc_sbuf_tensor("bdts", [128, KI], F32)
    dskip = nc.alloc_sbuf_tensor("dskips", [128, KI], F32)
    b1c = nc.alloc_sbuf_tensor("b1c", [128, NE, NDFF], F32)
    b2c = nc.alloc_sbuf_tensor("b2c", [128, NE, NDFF], F32)
    b3c = nc.alloc_sbuf_tensor("b3c", [128, NE, KD], F32)
    wrf = nc.alloc_sbuf_tensor("wrf", [128, KD, NE], F32)
    brc = nc.alloc_sbuf_tensor("brcs", [NE, 1], F32)
    brb = nc.alloc_sbuf_tensor("brb", [128, NE], F32)
    cst = nc.alloc_sbuf_tensor("cst", [128, NE], F32)
    csc = nc.alloc_sbuf_tensor("csc", [NE, 1], F32)
    csr = nc.alloc_sbuf_tensor("csr", [1, NE], F32)
    brr = nc.alloc_sbuf_tensor("brr", [1, NE], F32)
    kconst = nc.alloc_sbuf_tensor("kconst", [128, 2], F32)
    abBc = nc.alloc_sbuf_tensor("abBc", [DST, 512], F32)
    abCc = nc.alloc_sbuf_tensor("abCc", [DST, 512], F32)
    carry = nc.alloc_sbuf_tensor("carry", [128, KI, DST], F32)
    tail = nc.alloc_sbuf_tensor("tail", [128, KI, 4], F32)

    ARENA = 128 * KB
    arena = nc.alloc_sbuf_tensor("arena", [128, ARENA], mybir.dt.uint8)
    arf = arena.ap().bitcast(F32)
    arb = arena.ap().bitcast(BF16)

    def carve(off, shape, dtype):
        p = shape[0]
        n = int(np.prod(shape[1:]))
        esz = 4 if dtype == F32 else 2
        assert off % esz == 0 and off + n * esz <= ARENA, (off, shape)
        v = (arf if dtype == F32 else arb)[0:p, off // esz: off // esz + n]
        if len(shape) == 2:
            return v
        if len(shape) == 3:
            return v.rearrange("p (a b) -> p a b", a=shape[1], b=shape[2])
        return v.rearrange("p (a b c) -> p a b c", a=shape[1], b=shape[2], c=shape[3])

    aa_xn = carve(0, [128, KD, TC], BF16)
    aa_wq = carve(8 * KB, [128, KD, 512], BF16)
    aa_wkv = carve(16 * KB, [128, KD, 1024], BF16)
    aa_qraw = carve(32 * KB, [128, 512], F32)
    aa_sq = carve(34 * KB, [128, 512], F32)
    aa_sd = carve(36 * KB, [2, 512], F32)
    aa_rf = carve(38 * KB, [2, 512], F32)
    aa_pT = carve(0, [128, NTT, 512], BF16)
    aa_rs = carve(16 * KB, [1, 512], F32)
    aa_rfr = carve(18 * KB, [1, 512], F32)
    aa_rb = carve(20 * KB, [128, 512], F32)
    aa_ln_sq = [carve(32 * KB, [128, 512], F32), carve(34 * KB, [128, 512], F32)]
    aa_ln_rows = [carve(36 * KB, [1, 512], F32), carve(38 * KB, [1, 512], F32),
                  carve(113 * KB, [1, 512], F32)]
    aa_wo = carve(40 * KB, [128, 4, 1024], BF16)
    aa_qhat = carve(48 * KB, [128, 4, T], BF16)
    aa_khat = carve(64 * KB, [128, 4, T], BF16)
    aa_vaug = carve(80 * KB, [128, NTT, NH, 65], BF16)
    aa_outT = carve(97 * KB, [128, 4, T], BF16)

    cv_ld = [carve(115 * KB + j * 4 * KB, [128, 1024], F32) for j in range(2)]
    cv_bf = [carve(123 * KB + j * 2 * KB, [128, 1024], BF16) for j in range(2)]

    ff_xn = carve(0, [128, KD, TC], BF16)
    ff_h = carve(8 * KB, [128, NDFF, TC], BF16)
    ff_y = carve(40 * KB, [128, KD, TC], F32)
    ff_w1 = [carve(56 * KB + j * 4 * KB, [128, KD, 256], BF16) for j in range(2)]
    ff_w2 = [carve(64 * KB + j * 4 * KB, [128, KD, 256], BF16) for j in range(2)]
    ff_w3 = [carve(72 * KB + j * 4 * KB, [128, 2, 1024], BF16) for j in range(2)]
    ff_cwb = carve(80 * KB, [128, TC], F32)
    ff_cwT = carve(82 * KB, [NE, TC], F32)
    ff_lgT = carve(84 * KB, [NE, TC], F32)
    ff_lg = carve(86 * KB, [128, NE], F32)
    ff_lgm = carve(86 * KB + 64, [128, NE], F32)
    ff_cw = carve(86 * KB + 128, [128, NE], F32)
    ff_m = carve(86 * KB + 192, [128, 8], F32)
    ff_acol = carve(86 * KB + 256, [128, 1], F32)
    ff_bcol = carve(86 * KB + 272, [128, 1], F32)
    ff_t1 = [carve(88 * KB + j * 2 * KB, [128, 512], F32) for j in range(2)]
    ff_t2 = [carve(92 * KB + j * 2 * KB, [128, 512], F32) for j in range(2)]
    ff_sl = [carve(96 * KB + j * KB, [128, 512], BF16) for j in range(2)]
    ff_b2 = [carve(98 * KB + j * KB, [128, 512], BF16) for j in range(2)]
    ff_ln_sq = [carve(100 * KB, [128, 512], F32), carve(102 * KB, [128, 512], F32)]
    ff_ln_rows = [carve(104 * KB, [1, 512], F32), carve(106 * KB, [1, 512], F32),
                  carve(108 * KB, [1, 512], F32)]

    ab_selb = carve(0, [DST, DST * 128], F32)
    ab_wxT = carve(8 * KB, [128, KI, DTR + 2 * DST], BF16)
    ab_wdtT = carve(11 * KB, [DTR, DI], BF16)
    ab_dt = carve(16 * KB, [128, KI, TC], F32)
    ab_zs = carve(48 * KB, [128, KI, TC], BF16)
    ab_y = carve(64 * KB, [128, KI, TC], F32)
    ab_xi = carve(96 * KB, [128, KI, TC], BF16)
    ab_xn = carve(112 * KB, [128, KD, TC], BF16)
    ab_wing = [carve(120 * KB + j * 4 * KB, [128, KD, 256], BF16) for j in range(2)]
    ab_xiraw = [carve(64 * KB + j * 2560, [128, 528], F32) for j in range(2)]
    ab_cv = [carve(70 * KB + j * 2 * KB, [128, 512], F32) for j in range(2)]
    ab_spl = [carve(74 * KB + j * 2 * KB, [128, 512], F32) for j in range(2)]
    ab_dtr = carve(78 * KB, [DTR, 512], BF16)

    ab_ln_sq = [carve(64 * KB, [128, 512], F32), carve(66 * KB, [128, 512], F32)]
    ab_ln_rows = [carve(74 * KB, [1, 512], F32), carve(76 * KB, [1, 512], F32),
                  carve(83 * KB, [1, 512], F32)]
    ab_Bb = carve(112 * KB, [128, TC], F32)
    ab_Cb = carve(114 * KB, [128, TC], F32)
    ab_dApre = [carve(116 * KB, [128, TC], F32)]
    ab_dA = [carve(118 * KB, [128, TC], F32)]
    ab_h = [carve(120 * KB + j * 2 * KB, [128, TC], F32) for j in range(2)]
    ab_dBx = [carve(124 * KB + j * 2 * KB, [128, TC], F32) for j in range(2)]
    ab_yz = ab_xi
    ab_wog = [carve(16 * KB + j * 8 * KB, [128, KI, 256], BF16) for j in range(2)]

    pb = [nc.alloc_psum_tensor(f"pb{i}", [128, 512], F32) for i in range(8)]
    bankf = {i: None for i in range(8)}

    sLD = Sem(nc, "sLD")
    sW12 = [Sem(nc, "sW12a"), Sem(nc, "sW12b")]
    sW3 = [Sem(nc, "sW3a"), Sem(nc, "sW3b")]
    sWing = [Sem(nc, "sWinga"), Sem(nc, "sWingb")]
    sWog = [Sem(nc, "sWoga"), Sem(nc, "sWogb")]
    sAAW = Sem(nc, "sAAW")
    sLDc = [Sem(nc, "sLDc0"), Sem(nc, "sLDc1")]
    sSTc = [Sem(nc, "sSTc0"), Sem(nc, "sSTc1")]
    sPE = Sem(nc, "sPE")
    sVE = Sem(nc, "sVE")
    sSE = Sem(nc, "sSE")
    sGP = Sem(nc, "sGP")
    sOUT = Sem(nc, "sOUT")

    qq = {"sy": [], "pe": [], "ve": [], "se": [], "gp": []}

    def Q(eng, fn):
        qq[eng].append(fn)

    def W(eng, mark):
        if mark is None:
            return
        sem, v = mark
        Q(eng, lambda e, s=sem, v=v: e.wait_ge(s.h, v))

    def DR(eng):
        Q(eng, lambda e: e.drain())

    def OP(eng, sem, builder, k=1):
        if sem is None:
            Q(eng, builder)
            return None
        sem.n += k
        v = sem.n
        Q(eng, lambda e, b=builder, s=sem, kk=k: b(e).then_inc(s.h, kk))
        return (sem, v)

    def bank_wait(eng, *banks):
        for i in banks:
            W(eng, bankf[i])

    ld_xt = OP("sy", sLD, lambda e: e.dma_start(
        out=xT[:, :, :], in_=xT_h.ap().rearrange("(k p) t -> p k t", p=128)), 16)
    m_pro = ld_xt
    for dst, src in [
        (ident[:, :], ident_h[:, :]), (ones[:, :], ones_h[:, :]),
        (sel2[:, :], sel2_h[:, :]), (blk2[:, :], blk2_h[:, :]),
        (sel8[:, :], sel8_h[:, :]), (A_sb[:, :, :], alog_h[:, :, :]),
        (convw[:, :, :], convw_h[:, :, :]), (convb[:, :], convb_h[:, :]),
        (bdt[:, :], bdt_h[:, :]), (dskip[:, :], dskip_h[:, :]),
        (b1c[:, :, :], b1_h[:, :, :]), (b2c[:, :, :], b2_h[:, :, :]),
        (b3c[:, :, :], b3_h[:, :, :]),
        (wrf[:, :, :], wr_h.ap().rearrange("(k p) e -> p k e", p=128)),
        (brc[:, :], brc_h[:, :]),
    ]:
        m_pro = OP("sy", sLD, lambda e, d=dst, s=src: e.dma_start(out=d, in_=s), 16)

    for dst, src in [
        (aa_wq[:, :, :], wq_h.ap().rearrange("(k p) m -> p k m", p=128)),
        (aa_wkv[:, :, :], wkv_h.ap().rearrange("(k p) m -> p k m", p=128)),
        (aa_wo[:, :, :], wo_h.ap().rearrange("(k p) m -> p k m", p=128)),
    ]:
        m_c = OP("gp", sAAW, lambda e, d=dst, s=src: e.dma_start(out=d, in_=s), 16)
    W("gp", m_c)
    m_aa_w = OP("gp", sGP, lambda e: e.nop())
    m_vaug1 = OP("gp", sGP, lambda e: e.memset(aa_vaug[:, :, :, :], 1.0))
    DR("gp")
    OP("gp", None, lambda e: e.memset(kconst[:, 0:1], -ATT_SCALE))
    DR("gp")
    m_kc = OP("gp", sGP, lambda e: e.memset(kconst[:, 1:2], LN_EPS))
    DR("gp")
    m_tailz = OP("gp", sGP, lambda e: e.memset(tail[:, :, :], 0.0))
    DR("gp")
    m_carryz = OP("gp", sGP, lambda e: e.memset(carry[:, :, :], 0.0))

    W("se", m_pro)
    m_aexp = OP("se", sSE, lambda e: e.activation(A_sb[:, :, :], A_sb[:, :, :], AF.Exp,
                                                  bias=0.0, scale=1.0))
    W("ve", m_aexp)
    m_aneg = OP("ve", sVE, lambda e: e.tensor_scalar_mul(A_sb[:, :, :], A_sb[:, :, :], -1.0))

    W("pe", m_pro)
    for k in range(KD):
        m_cs = OP("pe", sPE if k == KD - 1 else None,
                  lambda e, k=k: e.matmul(pb[0][0:NE, 0:1], wrf[:, k, :], ones[:, 0:1],
                                          start=(k == 0), stop=(k == KD - 1)))
    W("ve", m_cs)
    m_csc = OP("ve", sVE, lambda e: e.tensor_copy(csc[:, :], pb[0][0:NE, 0:1]))
    bankf[0] = m_csc
    W("pe", m_csc)
    m_trc = OP("pe", sPE, lambda e: e.transpose(pb[1][0:1, 0:NE], csc[:, :], ident[0:NE, 0:NE]))
    m_trb = OP("pe", sPE, lambda e: e.transpose(pb[2][0:1, 0:NE], brc[:, :], ident[0:NE, 0:NE]))
    W("ve", m_trb)
    m_csr = OP("ve", sVE, lambda e: e.tensor_copy(csr[:, :], pb[1][0:1, 0:NE]))
    DR("ve")
    m_brr = OP("ve", sVE, lambda e: e.tensor_copy(brr[:, :], pb[2][0:1, 0:NE]))
    bankf[1] = m_csr
    bankf[2] = m_brr
    W("pe", m_brr)
    m_bc1 = OP("pe", sPE, lambda e: e.matmul(pb[3][:, 0:NE], ones[0:1, :], csr[:, :],
                                             start=True, stop=True))
    m_bc2 = OP("pe", sPE, lambda e: e.matmul(pb[4][:, 0:NE], ones[0:1, :], brr[:, :],
                                             start=True, stop=True))
    W("ve", m_bc2)
    m_cst = OP("ve", sVE, lambda e: e.tensor_copy(cst[:, :], pb[3][:, 0:NE]))
    DR("ve")
    m_brb = OP("ve", sVE, lambda e: e.tensor_copy(brb[:, :], pb[4][:, 0:NE]))
    bankf[3] = m_cst
    bankf[4] = m_brb

    conv_jobs = []

    def add_jobs(src2d, dst2d, rows, cols, tag):
        for k in range((rows + 127) // 128):
            r0, r1 = k * 128, min((k + 1) * 128, rows)
            for c0_ in range(0, cols, 1024):
                c1_ = min(c0_ + 1024, cols)
                conv_jobs.append((src2d[r0:r1, c0_:c1_], dst2d[r0:r1, c0_:c1_],
                                  c1_ - c0_, tag))

    for e_ in range(NE):
        add_jobs(w1_h[e_], w1s[e_], D, DFF, ("e", e_))
        add_jobs(w2_h[e_], w2s[e_], D, DFF, ("e", e_))
        add_jobs(w3_h[e_], w3s[e_], DFF, D, ("e", e_))
    add_jobs(wxT_h, wxTs, DI, DTR + 2 * DST, ("wx",))
    add_jobs(wdtT_h, wdtTs, DTR, DI, ("wdt",))
    add_jobs(winT_h, winTs, D, 2 * DI, ("win",))
    add_jobs(woutT_h, woutTs, DI, D, ("wout",))
    conv_done = {}
    _cv = {"j": 0, "last_store": [None, None], "last_cv": [None, None]}

    def pump_conv(n=None):
        while _cv["j"] < len(conv_jobs) and (n is None or n > 0):
            j = _cv["j"]
            src, dst, cols, tag = conv_jobs[j]
            b = j % 2
            rows_ = src.shape[0]
            W("sy", _cv["last_cv"][b])
            m_l = OP("sy", sLDc[b], lambda e, s=src, d=cv_ld[b][0:rows_, 0:cols]:
                     e.dma_start(out=d, in_=s), 16)
            W("gp", m_l)
            W("gp", _cv["last_store"][b])
            m_cv = OP("gp", sGP, lambda e, b=b, rows_=rows_, cols=cols: e.tensor_copy(
                cv_bf[b][0:rows_, 0:cols], cv_ld[b][0:rows_, 0:cols]))
            _cv["last_cv"][b] = m_cv
            W("se", m_cv)
            m_s = OP("se", sSTc[b], lambda e, b=b, rows_=rows_, cols=cols, d=dst:
                     e.dma_start(out=d, in_=cv_bf[b][0:rows_, 0:cols]), 16)
            _cv["last_store"][b] = m_s
            conv_done[tag] = (_cv["last_store"][0], _cv["last_store"][1])
            _cv["j"] += 1
            if n is not None:
                n -= 1

    pump_conv(64)
    m_conv_all = None

    def emit_ln(c, xn_ap, sq, rows, pw_pe=(), pw_se=(), pw_ve=()):
        c0 = c * TC
        r_mean, r_var, r_tmp = rows
        for w_ in pw_se:
            W("se", w_)
        for w_ in pw_pe:
            W("pe", w_)
        bank_wait("pe", 4, 5)
        mm_m = []
        for k in range(KD):
            if k >= 2:
                W("se", mm_m[k - 2])
            m_sq = OP("se", sSE, lambda e, k=k: e.square(sq[k % 2][:, :], xT[:, k, c0:c0 + TC]))
            W("pe", m_sq)
            OP("pe", None, lambda e, k=k: e.matmul(pb[4][0:1, :], ones[:, 0:1],
                                                   xT[:, k, c0:c0 + TC],
                                                   start=(k == 0), stop=(k == KD - 1)))
            m = OP("pe", sPE, lambda e, k=k: e.matmul(pb[5][0:1, :], ones[:, 0:1],
                                                      sq[k % 2][:, :],
                                                      start=(k == 0), stop=(k == KD - 1)))
            mm_m.append(m)
        m_stats = m
        for w_ in pw_ve:
            W("ve", w_)
        W("ve", m_stats)
        OP("ve", None, lambda e: e.tensor_scalar_mul(r_mean[:, :], pb[4][0:1, :], 1.0 / D))
        OP("ve", None, lambda e: e.tensor_scalar_mul(r_var[:, :], pb[5][0:1, :], 1.0 / D))
        DR("ve")
        OP("ve", None, lambda e: e.tensor_tensor(r_tmp[:, :], r_mean[:, :], r_mean[:, :], ALU.mult))
        DR("ve")
        m_var = OP("ve", sVE, lambda e: e.tensor_tensor(r_var[:, :], r_var[:, :], r_tmp[:, :],
                                                        ALU.subtract))
        bankf[4] = m_var
        bankf[5] = m_var
        W("se", m_var)
        m_sd = OP("se", sSE, lambda e: e.activation(r_tmp[:, :], r_var[:, :], AF.Sqrt,
                                                    bias=kconst[0:1, 1:2], scale=1.0))
        W("ve", m_sd)
        OP("ve", None, lambda e: e.reciprocal(r_var[:, :], r_tmp[:, :]))
        DR("ve")
        m_rows = OP("ve", sVE, lambda e: e.scalar_tensor_tensor(
            r_tmp[:, :], r_mean[:, :], -1.0, r_var[:, :], ALU.mult, ALU.mult))
        W("pe", m_rows)
        bank_wait("pe", 6, 7)
        OP("pe", None, lambda e: e.matmul(pb[6][:, :], ones[0:1, :], r_var[:, :],
                                          start=True, stop=True))
        m_bc = OP("pe", sPE, lambda e: e.matmul(pb[7][:, :], ones[0:1, :], r_tmp[:, :],
                                                start=True, stop=True))
        W("ve", m_bc)
        for k in range(KD):
            OP("ve", None, lambda e, k=k: e.tensor_tensor(sq[k % 2][:, :],
                                                          xT[:, k, c0:c0 + TC],
                                                          pb[6][:, :], ALU.mult))
            DR("ve")
            m_xn = OP("ve", sVE, lambda e, k=k: e.tensor_tensor(xn_ap[:, k, :],
                                                                sq[k % 2][:, :],
                                                                pb[7][:, :], ALU.add))
        DR("ve")
        bankf[6] = m_xn
        bankf[7] = m_xn
        return m_xn, r_var, r_tmp, m_rows

    m_res = ld_xt
    if stage in ("attn_a", "ffn1", "attn_b", "full"):
        W("pe", m_pro)
        W("se", m_pro)
        W("se", (sGP, m_kc[1]))
        m_hat_last = None
        m_v_last = None
        m_ss_last = None
        vb = Rot(2)
        for c in range(NCH):
            pw_se = [m_hat_last] if m_hat_last else []
            m_xn, _, _, _ = emit_ln(c, aa_xn, aa_ln_sq, aa_ln_rows,
                                    pw_pe=([m_aa_w] if c == 0 else []), pw_se=pw_se)
            pump_conv(30)
            for which in ("q", "k"):
                for mq in range(4):
                    pump_conv(8)
                    W("pe", m_xn)
                    bank_wait("pe", 0)
                    for k in range(KD):
                        lhs = (aa_wq if which == "q" else aa_wkv)[:, k, mq * 128:(mq + 1) * 128]
                        m_pj = OP("pe", sPE if k == KD - 1 else None,
                                  lambda e, l=lhs, k=k: e.matmul(
                                      pb[0][:, :], l, aa_xn[:, k, :],
                                      start=(k == 0), stop=(k == KD - 1)))
                    W("ve", m_pj)
                    m_qraw = OP("ve", sVE, lambda e: e.tensor_copy(aa_qraw[:, :], pb[0][:, :]))
                    bankf[0] = m_qraw
                    W("se", m_qraw)
                    m_sq = OP("se", sSE, lambda e: e.square(aa_sq[:, :], aa_qraw[:, :]))
                    W("pe", m_sq)
                    bank_wait("pe", 1)
                    m_ss = OP("pe", sPE, lambda e: e.matmul(pb[1][0:2, :], blk2[:, :],
                                                            aa_sq[:, :], start=True, stop=True))
                    m_ss_last = m_ss
                    W("se", m_ss)
                    m_sd = OP("se", sSE, lambda e: e.activation(aa_sd[:, :], pb[1][0:2, :],
                                                                AF.Sqrt, bias=0.0, scale=1.0))
                    bankf[1] = m_sd
                    W("ve", m_sd)
                    OP("ve", None, lambda e: e.tensor_scalar_max(aa_sd[:, :], aa_sd[:, :], L2_EPS))
                    DR("ve")
                    m_rf = OP("ve", sVE, lambda e: e.reciprocal(aa_rf[:, :], aa_sd[:, :]))
                    W("pe", m_rf)
                    bank_wait("pe", 2)
                    m_rfb = OP("pe", sPE, lambda e: e.matmul(pb[2][:, :], sel2[:, :],
                                                             aa_rf[:, :], start=True, stop=True))
                    W("ve", m_rfb)
                    dst = (aa_qhat if which == "q" else aa_khat)[:, mq, c * TC:(c + 1) * TC]
                    m_hat = OP("ve", sVE, lambda e, d=dst: e.tensor_tensor(
                        d, aa_qraw[:, :], pb[2][:, :], ALU.mult))
                    DR("ve")
                    bankf[2] = m_hat
                    m_hat_last = m_hat
            for tt in range(c * 4, c * 4 + 4):
                bank = 3 if tt % 2 == 0 else 0
                W("pe", m_xn)
                if tt == 0:
                    W("pe", (sGP, m_vaug1[1]))
                bank_wait("pe", bank)
                for k in range(KD):
                    m_vmm = OP("pe", sPE if k == KD - 1 else None,
                               lambda e, k=k, tt=tt, b=bank: e.matmul(
                                   pb[b][:, :],
                                   aa_xn[:, k, (tt % 4) * 128:((tt % 4) + 1) * 128],
                                   aa_wkv[:, k, 512:1024],
                                   start=(k == 0), stop=(k == KD - 1)))
                W("ve", m_vmm)
                for h in range(NH):
                    m_vc = OP("ve", sVE if h == NH - 1 else None,
                              lambda e, h=h, tt=tt, b=bank: e.tensor_copy(
                                  aa_vaug[:, tt, h, 0:64], pb[b][:, h * 64:(h + 1) * 64]))
                DR("ve")
                bankf[bank] = m_vc
                m_v_last = m_vc

        pvdone = {}
        expm = {}
        for c in range(NCH):
            for h in range(NH):
                idx = c * NH + h
                r0 = (h % 2) * 64
                pump_conv(14)
                W("pe", m_hat_last)
                if idx == 0:
                    W("pe", m_v_last)
                else:
                    W("pe", pvdone[idx - 1])
                if idx == 0:
                    W("se", m_ss_last)
                else:
                    W("se", pvdone[idx - 1])
                for kt in range(NTT):
                    if kt >= 2:
                        W("pe", expm[(idx, kt - 2)])
                    m_qk = OP("pe", sPE, lambda e, kt=kt, h=h, c=c, r0=r0: e.matmul(
                        pb[4 + (kt % 2)][:, :],
                        aa_khat[r0:r0 + 64, h // 2, kt * 128:(kt + 1) * 128],
                        aa_qhat[r0:r0 + 64, h // 2, c * TC:(c + 1) * TC],
                        start=True, stop=True))
                    W("se", m_qk)
                    expm[(idx, kt)] = OP("se", sSE, lambda e, kt=kt: e.activation(
                        aa_pT[:, kt, :], pb[4 + (kt % 2)][:, :], AF.Exp,
                        bias=kconst[:, 0:1], scale=ATT_SCALE))
                for kt in range(NTT):
                    W("pe", expm[(idx, kt)])
                    OP("pe", None, lambda e, kt=kt, h=h, r0=r0: e.matmul(
                        pb[6][r0:r0 + 64, :], aa_vaug[:, kt, h, 0:64], aa_pT[:, kt, :],
                        start=(kt == 0), stop=(kt == NTT - 1)))
                    m_pv = OP("pe", sPE if kt == NTT - 1 else None,
                              lambda e, kt=kt, h=h: e.matmul(
                                  pb[7][0:1, :], aa_vaug[:, kt, h, 64:65], aa_pT[:, kt, :],
                                  start=(kt == 0), stop=(kt == NTT - 1)))
                W("ve", m_pv)
                OP("ve", None, lambda e: e.tensor_scalar_max(aa_rs[:, :], pb[7][0:1, :], ATT_EPS))
                DR("ve")
                m_rcp = OP("ve", sVE, lambda e: e.reciprocal(aa_rfr[:, :], aa_rs[:, :]))
                W("pe", m_rcp)
                bank_wait("pe", 3)
                m_rcb = OP("pe", sPE, lambda e, r0=r0: e.matmul(
                    pb[3][r0:r0 + 64, :], ones[0:1, 0:64], aa_rfr[:, :], start=True, stop=True))
                W("ve", m_rcb)
                OP("ve", None, lambda e, r0=r0: e.tensor_copy(aa_rb[r0:r0 + 64, :],
                                                              pb[3][r0:r0 + 64, :]))
                DR("ve")
                m_eo = OP("ve", sVE, lambda e, r0=r0, h=h, c=c: e.tensor_tensor(
                    aa_outT[r0:r0 + 64, h // 2, c * TC:(c + 1) * TC],
                    pb[6][r0:r0 + 64, :], aa_rb[r0:r0 + 64, :], ALU.mult))
                DR("ve")
                pvdone[idx] = m_eo
                bankf[3] = m_eo
        for i in range(8):
            bankf[i] = pvdone[NCH * NH - 1]

        for c in range(NCH):
            for md in range(KD):
                bank = md % 2
                W("pe", pvdone[c * NH + NH - 1])
                bank_wait("pe", bank)
                for ki in range(4):
                    m_wo = OP("pe", sPE if ki == 3 else None,
                              lambda e, ki=ki, md=md, c=c, b=bank: e.matmul(
                                  pb[b][:, :], aa_wo[:, ki, md * 128:(md + 1) * 128],
                                  aa_outT[:, ki, c * TC:(c + 1) * TC],
                                  start=(ki == 0), stop=(ki == 3)))
                W("ve", m_wo)
                m_add = OP("ve", sVE, lambda e, md=md, c=c, b=bank: e.tensor_tensor(
                    xT[:, md, c * TC:(c + 1) * TC], pb[b][:, :],
                    xT[:, md, c * TC:(c + 1) * TC], ALU.add))
                DR("ve")
                bankf[bank] = m_add
        m_res = m_add

    ffn_state = {"w12": Rot(2), "w3": Rot(2), "sl": Rot(2), "t1": Rot(2),
                 "h_last2": [None, None]}

    def emit_ffn(pidx, m_prev):
        rw12, rw3 = ffn_state["w12"], ffn_state["w3"]
        rsl, rt1 = ffn_state["sl"], ffn_state["t1"]
        for j in (0, 1):
            rw12.last[j] = m_prev
            rw3.last[j] = m_prev
        m_last_vadd = m_prev
        for c in range(DBG_FFN_CHUNKS or NCH):
            c0 = c * TC
            m_xn, r_rstd, r_brow, m_rows = emit_ln(
                c, ff_xn, ff_ln_sq, ff_ln_rows,
                pw_pe=[m_prev], pw_se=[m_prev, m_last_vadd], pw_ve=[m_last_vadd])
            if c > 0 and DBG_FFN_PART == "ln":
                break
            bank_wait("pe", 0)
            W("pe", m_prev)
            for k in range(KD):
                m_lg = OP("pe", sPE if k == KD - 1 else None,
                          lambda e, k=k, c0=c0: e.matmul(pb[0][0:NE, :], wrf[:, k, :],
                                                         xT[:, k, c0:c0 + TC],
                                                         start=(k == 0), stop=(k == KD - 1)))
            W("ve", m_lg)
            m_lgT = OP("ve", sVE, lambda e: e.tensor_copy(ff_lgT[:, :], pb[0][0:NE, :]))
            DR("ve")
            bankf[0] = m_lgT
            m_cT = None
            for tt in range(4):
                W("pe", m_lgT)
                W("pe", m_rows)
                bank_wait("pe", 1, 2, 3)
                OP("pe", None, lambda e, tt=tt: e.transpose(
                    pb[1][0:128, 0:NE], ff_lgT[0:NE, tt * 128:(tt + 1) * 128],
                    ident[0:NE, 0:NE]))
                OP("pe", None, lambda e, tt=tt, r=r_rstd: e.matmul(
                    pb[2][:, 0:1], r[:, tt * 128:(tt + 1) * 128], ones[0:1, 0:1],
                    start=True, stop=True))
                m_t3 = OP("pe", sPE, lambda e, tt=tt, r=r_brow: e.matmul(
                    pb[3][:, 0:1], r[:, tt * 128:(tt + 1) * 128], ones[0:1, 0:1],
                    start=True, stop=True))
                W("ve", m_t3)
                OP("ve", None, lambda e: e.tensor_copy(ff_lg[:, :], pb[1][0:128, 0:NE]))
                OP("ve", None, lambda e: e.tensor_copy(ff_acol[:, :], pb[2][:, 0:1]))
                m_cp = OP("ve", sVE, lambda e: e.tensor_copy(ff_bcol[:, :], pb[3][:, 0:1]))
                bankf[1] = m_cp
                bankf[2] = m_cp
                bankf[3] = m_cp
                DR("ve")
                OP("ve", None, lambda e: e.scalar_tensor_tensor(
                    ff_lg[:, :], ff_lg[:, :], ff_acol[:, 0:1], brb[:, :], ALU.mult, ALU.add))
                DR("ve")
                OP("ve", None, lambda e: e.scalar_tensor_tensor(
                    ff_lg[:, :], cst[:, :], ff_bcol[:, 0:1], ff_lg[:, :], ALU.mult, ALU.add))
                DR("ve")
                OP("ve", None, lambda e: e.reduce_max(ff_m[:, 0:1], ff_lg[:, :], axis=AX.X))
                DR("ve")
                OP("ve", None, lambda e: e.tensor_scalar(ff_cw[:, :], ff_lg[:, :],
                                                         ff_m[:, 0:1], None, ALU.is_ge))
                DR("ve")
                OP("ve", None, lambda e: e.scalar_tensor_tensor(
                    ff_lgm[:, :], ff_cw[:, :], -1e30, ff_lg[:, :], ALU.mult, ALU.add))
                DR("ve")
                m_m2 = OP("ve", sVE, lambda e: e.reduce_max(ff_m[:, 1:2], ff_lgm[:, :], axis=AX.X))
                W("se", m_m2)
                OP("se", None, lambda e: e.activation(ff_m[:, 2:3], ff_m[:, 1:2], AF.Copy,
                                                      bias=0.0, scale=-1.0))
                DR("se")
                OP("se", None, lambda e: e.activation(ff_m[:, 3:4], ff_m[:, 0:1], AF.Sigmoid,
                                                      bias=ff_m[:, 2:3], scale=1.0))
                DR("se")
                m_w2s = OP("se", sSE, lambda e: e.activation(ff_m[:, 4:5], ff_m[:, 3:4], AF.Copy,
                                                             bias=1.0, scale=-1.0))
                W("ve", m_w2s)
                OP("ve", None, lambda e: e.tensor_scalar(ff_lgm[:, :], ff_lgm[:, :],
                                                         ff_m[:, 1:2], None, ALU.is_ge))
                OP("ve", None, lambda e: e.tensor_scalar(ff_cw[:, :], ff_cw[:, :],
                                                         ff_m[:, 3:4], None, ALU.mult))
                DR("ve")
                m_cw = OP("ve", sVE, lambda e: e.scalar_tensor_tensor(
                    ff_cw[:, :], ff_lgm[:, :], ff_m[:, 4:5], ff_cw[:, :], ALU.mult, ALU.add))
                DR("ve")
                W("pe", m_cw)
                bank_wait("pe", 1)
                m_ctr = OP("pe", sPE, lambda e: e.transpose(pb[1][0:NE, 0:128], ff_cw[:, :],
                                                            ident[:, :]))
                W("ve", m_ctr)
                m_cT = OP("ve", sVE, lambda e, tt=tt: e.tensor_copy(
                    ff_cwT[:, tt * 128:(tt + 1) * 128], pb[1][0:NE, 0:128]))
                DR("ve")
                bankf[1] = m_cT
            m_router = m_cT
            if c > 0 and DBG_FFN_PART == "router":
                break

            for e_ in range(NE):
                if e_ == 0:
                    W("pe", m_router)
                h_marks = []
                for g in range(16):
                    j = g % 2
                    if g == 8:
                        bank_wait("pe", 4)
                        m_cwbmm = OP("pe", sPE, lambda e, e_=e_: e.matmul(
                            pb[4][:, :], sel8[:, e_ * 128:(e_ + 1) * 128], ff_cwT[0:NE, :],
                            start=True, stop=True))
                        W("ve", m_cwbmm)
                        m_cwb = OP("ve", sVE, lambda e: e.tensor_copy(ff_cwb[:, :], pb[4][:, :]))
                        DR("ve")
                        bankf[4] = m_cwb
                    rw12.wait(W, "sy", j)
                    if pidx == 1 and c == 0:
                        for m_ in conv_done[("e", e_)]:
                            W("sy", m_)
                    OP("sy", sW12[j], lambda e, e_=e_, g=g, j=j: e.dma_start(
                        out=ff_w1[j][:, :, :],
                        in_=w1s[e_, :, g * 256:(g + 1) * 256].rearrange(
                            "(k p) m -> p k m", p=128)), 16)
                    m_l = OP("sy", sW12[j], lambda e, e_=e_, g=g, j=j: e.dma_start(
                        out=ff_w2[j][:, :, :],
                        in_=w2s[e_, :, g * 256:(g + 1) * 256].rearrange(
                            "(k p) m -> p k m", p=128)), 16)
                    for dt2 in range(2):
                        dff = g * 2 + dt2
                        bb = 2 * dt2
                        W("pe", m_l)
                        W("pe", m_xn)
                        if len(h_marks) >= 2:
                            W("pe", h_marks[-2])
                        else:
                            bank_wait("pe", bb, bb + 1)
                        for k in range(KD):
                            OP("pe", None, lambda e, k=k, j=j, dt2=dt2, b=bb: e.matmul(
                                pb[b][:, :], ff_w1[j][:, k, dt2 * 128:(dt2 + 1) * 128],
                                ff_xn[:, k, :], start=(k == 0), stop=(k == KD - 1)))
                        for k in range(KD):
                            m_mm = OP("pe", sPE if k == KD - 1 else None,
                                      lambda e, k=k, j=j, dt2=dt2, b=bb: e.matmul(
                                          pb[b + 1][:, :],
                                          ff_w2[j][:, k, dt2 * 128:(dt2 + 1) * 128],
                                          ff_xn[:, k, :], start=(k == 0), stop=(k == KD - 1)))
                        W("se", m_mm)
                        rsl.wait(W, "se", dff % 2)
                        m_sl = OP("se", sSE, lambda e, e_=e_, dff=dff, b=bb: e.activation(
                            ff_sl[dff % 2][:, :], pb[b][:, :], AF.Silu,
                            bias=b1c[:, e_, dff:dff + 1], scale=1.0))
                        W("ve", m_mm)
                        W("ve", m_sl)
                        m_h = OP("ve", sVE, lambda e, e_=e_, dff=dff, b=bb: e.scalar_tensor_tensor(
                            ff_h[:, dff, :], pb[b + 1][:, :], b2c[:, e_, dff:dff + 1],
                            ff_sl[dff % 2][:, :], ALU.add, ALU.mult))
                        DR("ve")
                        h_marks.append(m_h)
                        rsl.set(dff % 2, m_h)
                        if dt2 == 1:
                            rw12.set(j, m_mm)
                W("pe", h_marks[-1])
                W("pe", h_marks[-2])
                bank_wait("pe", 5, 6, 7)
                W("pe", m_cwb)
                for gg in range(16):
                    j = gg % 2
                    rw3.wait(W, "sy", j)
                    m_wl = OP("sy", sW3[j], lambda e, e_=e_, gg=gg, j=j: e.dma_start(
                        out=ff_w3[j][:, :, :],
                        in_=w3s[e_, gg * 256:(gg + 1) * 256, :].rearrange(
                            "(k p) m -> p k m", p=128)), 16)
                    W("pe", m_wl)
                    W("pe", h_marks[gg * 2 + 1])
                    for md in range(KD):
                        for jj in range(2):
                            m_w3 = OP("pe",
                                      sPE if (md == KD - 1 and jj == 1) else None,
                                      lambda e, gg=gg, j=j, jj=jj, md=md: e.matmul(
                                          pb[md][:, :],
                                          ff_w3[j][:, jj, md * 128:(md + 1) * 128],
                                          ff_h[:, gg * 2 + jj, :],
                                          start=(gg == 0 and jj == 0),
                                          stop=(gg == 15 and jj == 1)))
                    rw3.set(j, m_w3)
                for md in range(KD):
                    W("se", m_w3)
                    rt1.wait(W, "se", md % 2)
                    m_t1e = OP("se", sSE, lambda e, e_=e_, md=md: e.activation(
                        ff_t1[md % 2][:, :], pb[md][:, :], AF.Identity,
                        bias=b3c[:, e_, md:md + 1], scale=1.0))
                    bankf[md] = m_t1e
                    if e_ == 0:
                        W("ve", m_t1e)
                        W("ve", m_cwb)
                        m_ya = OP("ve", sVE, lambda e, md=md: e.tensor_tensor(
                            ff_y[:, md, :], ff_t1[md % 2][:, :], ff_cwb[:, :], ALU.mult))
                        DR("ve")
                    else:
                        W("gp", m_t1e)
                        W("gp", m_cwb)
                        rt1.wait(W, "gp", md % 2)
                        m_t2 = OP("gp", sGP, lambda e, md=md: e.tensor_tensor(
                            ff_t2[md % 2][:, :], ff_t1[md % 2][:, :], ff_cwb[:, :], ALU.mult))
                        W("ve", m_t2)
                        m_ya = OP("ve", sVE, lambda e, md=md: e.tensor_tensor(
                            ff_y[:, md, :], ff_y[:, md, :], ff_t2[md % 2][:, :], ALU.add))
                        DR("ve")
                    rt1.set(md % 2, m_ya)
            for md in range(KD):
                m_vadd = OP("ve", sVE, lambda e, md=md, c0=c0: e.tensor_tensor(
                    xT[:, md, c0:c0 + TC], ff_y[:, md, :], xT[:, md, c0:c0 + TC], ALU.add))
                DR("ve")
            m_last_vadd = m_vadd
        return m_last_vadd

    pump_conv(None)
    m_conv_all = conv_done[("wout",)]

    if stage in ("ffn1", "attn_b", "full"):
        m_res = emit_ffn(1, m_res)

    if stage in ("attn_b", "full"):
        for m_ in m_conv_all:
            W("sy", m_)
        W("sy", m_res)
        m_selb = OP("sy", sLD, lambda e: e.dma_start(out=ab_selb[:, :], in_=selb_h[:, :]), 16)
        OP("sy", sLD, lambda e: e.dma_start(
            out=ab_wxT[:, :, :], in_=wxTs.ap().rearrange("(k p) m -> p k m", p=128)), 16)
        m_wx = OP("sy", sLD, lambda e: e.dma_start(out=ab_wdtT[:, :], in_=wdtTs[:, :]), 16)
        m_wdt = m_wx
        m_selb = m_wx

        g_exp = {}
        g_scan = {}
        g_scan_it = {}
        g_yadd = {}
        g_dbx = {}
        g_hc = {}
        git = [0]
        rwing = Rot(2)
        rwog = Rot(2)
        for j in (0, 1):
            rwing.last[j] = m_res
            rwog.last[j] = m_res
        scan_rel = [None, None]
        m_res_ve = m_res
        for c in range(NCH):
            c0 = c * TC
            m_xn, _, _, _ = emit_ln(c, ab_xn, ab_ln_sq, ab_ln_rows,
                                    pw_pe=[m_res], pw_se=[m_res, m_res_ve],
                                    pw_ve=[m_conv_all[0], m_conv_all[1], m_res_ve])
            xz_m = {}
            for g in range(16):
                j = g % 2
                rwing.wait(W, "sy", j)
                if g == 0:
                    W("sy", scan_rel[0])
                    W("sy", scan_rel[1])
                if c == 0 and g == 0:
                    for m_ in conv_done[("win",)]:
                        W("sy", m_)
                m_l = OP("sy", sWing[j], lambda e, g=g, j=j: e.dma_start(
                    out=ab_wing[j][:, :, :],
                    in_=winTs[:, g * 256:(g + 1) * 256].rearrange("(k p) m -> p k m", p=128)), 16)
                for m2 in range(2):
                    mi = g * 2 + m2
                    bb = mi % 2
                    W("pe", m_l)
                    W("pe", m_xn)
                    bank_wait("pe", bb)
                    for k in range(KD):
                        m_mm = OP("pe", sPE if k == KD - 1 else None,
                                  lambda e, k=k, j=j, m2=m2, b=bb: e.matmul(
                                      pb[b][:, :], ab_wing[j][:, k, m2 * 128:(m2 + 1) * 128],
                                      ab_xn[:, k, :], start=(k == 0), stop=(k == KD - 1)))
                    if m2 == 1:
                        rwing.set(j, m_mm)
                    if mi < KI:
                        W("ve", m_mm)
                        if c == 0 and mi == 0:
                            W("ve", (sGP, m_tailz[1]))
                        xr = ab_xiraw[mi % 2]
                        OP("ve", None, lambda e, mi=mi, xr=xr: e.tensor_copy(
                            xr[:, 0:3], tail[:, mi, 0:3]))
                        DR("ve")
                        m_xr = OP("ve", sVE, lambda e, xr=xr, b=bb: e.tensor_copy(
                            xr[:, 3:515], pb[b][:, :]))
                        bankf[bb] = m_xr
                        DR("ve")
                        OP("ve", None, lambda e, mi=mi, xr=xr: e.tensor_copy(
                            tail[:, mi, 0:3], xr[:, 512:515]))
                        DR("ve")
                        cv = ab_cv[mi % 2]
                        if mi >= 2:
                            W("ve", xz_m[mi - 2])
                        OP("ve", None, lambda e, mi=mi, xr=xr, cv=cv: e.tensor_scalar(
                            cv[:, :], xr[:, 0:512], convw[:, mi, 0:1], None, ALU.mult))
                        DR("ve")
                        for jj in range(1, 3):
                            OP("ve", None, lambda e, mi=mi, xr=xr, cv=cv, jj=jj:
                               e.scalar_tensor_tensor(cv[:, :], xr[:, jj:jj + 512],
                                                      convw[:, mi, jj:jj + 1], cv[:, :],
                                                      ALU.mult, ALU.add))
                            DR("ve")
                        m_cv = OP("ve", sVE, lambda e, mi=mi, xr=xr, cv=cv:
                                  e.scalar_tensor_tensor(cv[:, :], xr[:, 3:515][:, 0:512],
                                                         convw[:, mi, 3:4], cv[:, :],
                                                         ALU.mult, ALU.add))
                        DR("ve")
                        W("se", m_cv)
                        m_xi = OP("se", sSE, lambda e, mi=mi, cv=cv: e.activation(
                            ab_xi[:, mi, :], cv[:, :], AF.Silu,
                            bias=convb[:, mi:mi + 1], scale=1.0))
                        xz_m[mi] = m_xi
                    else:
                        W("se", m_mm)
                        m_z = OP("se", sSE, lambda e, mi=mi, b=bb: e.activation(
                            ab_zs[:, mi - KI, :], pb[b][:, :], AF.Silu, bias=0.0, scale=1.0))
                        bankf[bb] = m_z
                        xz_m[mi] = m_z
            m_z_all = xz_m[2 * KI - 1]

            if c == 0:
                W("pe", m_wx)
            bank_wait("pe", 2, 3, 4)
            for mi in range(KI):
                W("pe", xz_m[mi])
                OP("pe", None, lambda e, mi=mi: e.matmul(
                    pb[2][0:DTR, :], ab_wxT[:, mi, 0:DTR], ab_xi[:, mi, :],
                    start=(mi == 0), stop=(mi == KI - 1)))
                OP("pe", None, lambda e, mi=mi: e.matmul(
                    pb[3][0:DST, :], ab_wxT[:, mi, DTR:DTR + DST], ab_xi[:, mi, :],
                    start=(mi == 0), stop=(mi == KI - 1)))
                m_xd = OP("pe", sPE if mi == KI - 1 else None,
                          lambda e, mi=mi: e.matmul(
                              pb[4][0:DST, :], ab_wxT[:, mi, DTR + DST:DTR + 2 * DST],
                              ab_xi[:, mi, :], start=(mi == 0), stop=(mi == KI - 1)))
            W("ve", m_xd)
            m_dtr = OP("ve", sVE, lambda e: e.tensor_copy(ab_dtr[:, :], pb[2][0:DTR, :]))
            DR("ve")
            bankf[2] = m_dtr
            W("se", m_xd)
            OP("se", None, lambda e: e.copy(abBc[:, :], pb[3][0:DST, :]))
            DR("se")
            m_BC = OP("se", sSE, lambda e: e.copy(abCc[:, :], pb[4][0:DST, :]))
            bankf[3] = m_BC
            bankf[4] = m_BC

            if c == 0:
                W("pe", m_wdt)
            W("pe", m_dtr)
            dt_m = []
            for mi in range(KI):
                bb = 5 + mi % 2
                bank_wait("pe", bb)
                m_dp = OP("pe", sPE, lambda e, mi=mi, b=bb: e.matmul(
                    pb[b][:, :], ab_wdtT[:, mi * 128:(mi + 1) * 128], ab_dtr[:, :],
                    start=True, stop=True))
                W("ve", m_dp)
                v = ab_spl[mi % 2]
                av = ab_cv[mi % 2]
                m_v0 = OP("ve", sVE, lambda e, mi=mi, v=v, b=bb: e.tensor_scalar(
                    v[:, :], pb[b][:, :], bdt[:, mi:mi + 1], None, ALU.add))
                bankf[bb] = m_v0
                DR("ve")
                OP("ve", None, lambda e, v=v, av=av: e.tensor_scalar(
                    av[:, :], v[:, :], -1.0, None, ALU.mult))
                DR("ve")
                m_av = OP("ve", sVE, lambda e, v=v, av=av: e.tensor_tensor(
                    av[:, :], v[:, :], av[:, :], ALU.max))
                W("se", m_av)
                OP("se", None, lambda e, av=av: e.activation(av[:, :], av[:, :], AF.Exp,
                                                             bias=0.0, scale=-1.0))
                DR("se")
                m_l1p = OP("se", sSE, lambda e, av=av: e.activation(av[:, :], av[:, :], AF.Ln,
                                                                   bias=1.0, scale=1.0))
                DR("ve")
                OP("ve", None, lambda e, v=v: e.tensor_scalar_max(v[:, :], v[:, :], 0.0))
                DR("ve")
                W("ve", m_l1p)
                m_dt = OP("ve", sVE, lambda e, mi=mi, v=v, av=av: e.tensor_tensor(
                    ab_dt[:, mi, :], v[:, :], av[:, :], ALU.add))
                DR("ve")
                dt_m.append(m_dt)
            m_dt_all = dt_m[-1]

            m_u = None
            W("ve", (sGP, sGP.n))
            for mi in range(KI):
                W("ve", dt_m[mi])
                OP("ve", None, lambda e, mi=mi: e.tensor_scalar(
                    ab_y[:, mi, :], ab_xi[:, mi, :], dskip[:, mi:mi + 1], None, ALU.mult))
                DR("ve")
                m_u = OP("ve", sVE, lambda e, mi=mi: e.tensor_tensor(
                    ab_xi[:, mi, :], ab_dt[:, mi, :], ab_xi[:, mi, :], ALU.mult))
                DR("ve")
            m_u_all = m_u
            ab_u = ab_xi

            if c == 0:
                W("ve", (sGP, m_carryz[1]))
                W("ve", m_aneg)
                W("se", m_aneg)
            W("se", m_dt_all)

            def emit_tail(itp):
                mip = itp % KI
                jbp = itp % 2
                g_hc[itp] = OP("ve", sVE, lambda e, jbp=jbp: e.tensor_tensor(
                    ab_dBx[jbp][:, :], ab_h[jbp][:, :], ab_Cb[:, :], ALU.mult))
                DR("ve")
                g_yadd[itp] = OP("ve", sVE, lambda e, mip=mip, jbp=jbp: e.tensor_tensor(
                    ab_y[:, mip, :], ab_y[:, mip, :], ab_dBx[jbp][:, :], ALU.add))
                DR("ve")

            for n in range(DST):
                it0 = git[0] * KI
                W("pe", m_BC)
                if c == 0 and n == 0:
                    W("pe", m_selb)
                bank_wait("pe", 7, 2)
                m_bbm = OP("pe", sPE, lambda e, n=n: e.matmul(
                    pb[7][:, :], ab_selb[:, n * 128:(n + 1) * 128], abBc[:, :],
                    start=True, stop=True))
                m_cbm = OP("pe", sPE, lambda e, n=n: e.matmul(
                    pb[2][:, :], ab_selb[:, n * 128:(n + 1) * 128], abCc[:, :],
                    start=True, stop=True))
                W("se", m_cbm)
                if it0 >= 1:
                    W("se", g_hc[it0 - 1])
                OP("se", None, lambda e: e.copy(ab_Bb[:, :], pb[7][:, :]))
                DR("se")
                m_bcb = OP("se", sSE, lambda e: e.copy(ab_Cb[:, :], pb[2][:, :]))
                bankf[7] = m_bcb
                bankf[2] = m_bcb
                W("gp", m_bcb)
                if n == 0:
                    W("gp", m_u_all)
                for mi0 in range(2):
                    itb = it0 + mi0
                    if itb >= 2:
                        W("gp", g_yadd[itb - 2])
                    g_dbx[itb] = OP("gp", sGP, lambda e, mi0=mi0, jb=itb % 2, itb=itb: e.tensor_tensor(
                        ab_dBx[jb][:, :], ab_u[:, (itb % KI), :], ab_Bb[:, :], ALU.mult))
                for mi in range(KI):
                    it = it0 + mi
                    jb = it % 2
                    if it >= 1:
                        W("se", g_scan_it[it - 1])
                        OP("se", None, lambda e, it=it: e.copy(
                            carry[:, (it - 1) % KI, (it - 1) // KI % DST:
                                  (it - 1) // KI % DST + 1],
                            ab_h[(it - 1) % 2][:, 511:512]))
                        DR("se")
                    g_exp[it] = OP("se", sSE, lambda e, mi=mi, n=n: e.activation(
                        ab_dA[0][:, :], ab_dt[:, mi, :], AF.Exp, bias=0.0,
                        scale=A_sb[:, mi, n:n + 1]))
                    if it == 0 and c == 0:
                        W("ve", m_u_all)
                    W("ve", g_exp[it])
                    W("ve", g_dbx[it])
                    m_sc = OP("ve", sVE, lambda e, mi=mi, n=n, jb=jb: e.tensor_tensor_scan(
                        ab_h[jb][:, :], ab_dA[0][:, :], ab_dBx[jb][:, :],
                        carry[:, mi, n:n + 1], ALU.mult, ALU.add))
                    g_scan_it[it] = m_sc
                    DR("ve")
                    if 1 <= mi <= KI - 2:
                        W("gp", g_yadd[it - 1])
                        g_dbx[it + 1] = OP("gp", sGP, lambda e, mi=mi, jb=(it + 1) % 2:
                                           e.tensor_tensor(ab_dBx[jb][:, :],
                                                           ab_u[:, mi + 1, :],
                                                           ab_Bb[:, :], ALU.mult))
                    emit_tail(it)
                git[0] += 1
            m_y = g_yadd[git[0] * KI - 1]
            itf = git[0] * KI - 1
            W("se", g_scan_it[itf])
            m_carryf = OP("se", sSE, lambda e, itf=itf: e.copy(
                carry[:, itf % KI, (itf // KI) % DST:(itf // KI) % DST + 1],
                ab_h[itf % 2][:, 511:512]))
            scan_rel = [m_carryf, m_y]

            m_yz = None
            W("ve", m_z_all)
            for mi in range(KI):
                m_yz = OP("ve", sVE, lambda e, mi=mi: e.tensor_tensor(
                    ab_yz[:, mi, :], ab_y[:, mi, :], ab_zs[:, mi, :], ALU.mult))
                DR("ve")
            for g2 in range(4):
                j = g2 % 2
                rwog.wait(W, "sy", j)
                W("sy", m_yz)
                if c == 0 and g2 == 0:
                    for m_ in conv_done[("wout",)]:
                        W("sy", m_)
                m_wl = OP("sy", sWog[j], lambda e, g2=g2, j=j: e.dma_start(
                    out=ab_wog[j][:, :, :],
                    in_=woutTs[:, g2 * 256:(g2 + 1) * 256].rearrange(
                        "(k p) m -> p k m", p=128)), 16)
                for m2 in range(2):
                    md = g2 * 2 + m2
                    bb = 3 + md % 2
                    W("pe", m_wl)
                    W("pe", m_yz)
                    bank_wait("pe", bb)
                    for kk in range(KI):
                        m_mm = OP("pe", sPE if kk == KI - 1 else None,
                                  lambda e, kk=kk, j=j, m2=m2, b=bb: e.matmul(
                                      pb[b][:, :], ab_wog[j][:, kk, m2 * 128:(m2 + 1) * 128],
                                      ab_yz[:, kk, :], start=(kk == 0), stop=(kk == KI - 1)))
                    if m2 == 1:
                        rwog.set(j, m_mm)
                    W("ve", m_mm)
                    m_vadd = OP("ve", sVE, lambda e, md=md, c0=c0, b=bb: e.tensor_tensor(
                        xT[:, md, c0:c0 + TC], pb[b][:, :], xT[:, md, c0:c0 + TC], ALU.add))
                    DR("ve")
                    bankf[bb] = m_vadd
            m_res_ve = m_vadd
        m_res = m_res_ve

    if stage == "full":
        m_res = emit_ffn(2, m_res)

    W("sy", (sVE, sVE.n))
    W("sy", (sSE, sSE.n))
    W("sy", (sPE, sPE.n))
    W("sy", (sGP, sGP.n))
    OP("sy", sOUT, lambda e: e.dma_start(
        out=outT_h.ap().rearrange("(k p) t -> p k t", p=128), in_=xT[:, :, :]), 16)
    W("sy", (sOUT, 16))

    with nc.Block() as block:
        block.sync(lambda e: [f(e) for f in qq["sy"]])
        block.tensor(lambda e: [f(e) for f in qq["pe"]])
        block.vector(lambda e: [f(e) for f in qq["ve"]])
        block.scalar(lambda e: [f(e) for f in qq["se"]])
        block.gpsimd(lambda e: [f(e) for f in qq["gp"]])
    return nc


def _prep_shared(i):
    f32 = np.float32
    sel8 = np.zeros((NE, NE * 128), f32)
    for e in range(NE):
        sel8[e, e * 128:(e + 1) * 128] = 1.0
    selb = np.zeros((DST, DST * 128), f32)
    for n in range(DST):
        selb[n, n * 128:(n + 1) * 128] = 1.0
    sel2 = np.zeros((2, 128), f32)
    sel2[0, 0:64] = 1.0
    sel2[1, 64:128] = 1.0
    blk2 = np.zeros((128, 2), f32)
    blk2[0:64, 0] = 1.0
    blk2[64:128, 1] = 1.0
    c = np.ascontiguousarray
    return {
        "wq": c(np.asarray(i["L0_attn_a_Wq"], f32)),
        "wkv": c(np.asarray(i["L0_attn_a_Wkv"], f32)),
        "wo": c(np.asarray(i["L0_attn_a_Wo"], f32)),
        "wr": c(np.asarray(i["L0_ffn_Wr"], f32)),
        "brc": c(np.asarray(i["L0_ffn_br"], f32).reshape(NE, 1)),
        "w1": c(np.asarray(i["L0_ffn_W1"], f32)),
        "w2": c(np.asarray(i["L0_ffn_W2"], f32)),
        "w3": c(np.asarray(i["L0_ffn_W3"], f32)),
        "b1": c(np.asarray(i["L0_ffn_b1"], f32).reshape(NE, NDFF, 128).transpose(2, 0, 1)),
        "b2": c(np.asarray(i["L0_ffn_b2"], f32).reshape(NE, NDFF, 128).transpose(2, 0, 1)),
        "b3": c(np.asarray(i["L0_ffn_b3"], f32).reshape(NE, KD, 128).transpose(2, 0, 1)),
        "winT": c(np.asarray(i["L0_attn_b_W_in"], f32).T),
        "wxT": c(np.asarray(i["L0_attn_b_W_x"], f32).T),
        "wdtT": c(np.asarray(i["L0_attn_b_W_dt"], f32).T),
        "woutT": c(np.asarray(i["L0_attn_b_W_out"], f32).T),
        "convw": c(np.asarray(i["L0_attn_b_conv_w"], f32)[:, 0, :]
                   .reshape(KI, 128, 4).transpose(1, 0, 2)),
        "convb": c(np.asarray(i["L0_attn_b_conv_b"], f32).reshape(KI, 128).T),
        "bdt": c(np.asarray(i["L0_attn_b_b_dt"], f32).reshape(KI, 128).T),
        "alog": c(np.asarray(i["L0_attn_b_A_log"], f32).reshape(KI, 128, DST)
                  .transpose(1, 0, 2)),
        "dskip": c(np.asarray(i["L0_attn_b_D_skip"], f32).reshape(KI, 128).T),
        "identc": np.eye(128, dtype=f32),
        "onesc": np.ones((128, 128), f32),
        "sel2c": sel2,
        "blk2c": blk2,
        "sel8c": sel8,
        "selbc": selb,
    }


def kernel(**inputs):
    x = np.asarray(inputs["L0_attn_a_x"], np.float32)
    B = x.shape[0]
    shared = _prep_shared(inputs)
    nc = build("full")
    in_maps = []
    for b in range(B):
        m = dict(shared)
        m["xT"] = np.ascontiguousarray(x[b].T)
        in_maps.append(m)
    res = run_bass_kernel_spmd(nc, in_maps, core_ids=list(range(B)))
    out = np.stack([np.asarray(r["outT"]).T for r in res.results], axis=0)
    return out.astype(np.float32)


# version 09
# speedup vs baseline: 1.0069x; 1.0069x over previous
import numpy as np
import ml_dtypes
import concourse.bass as bass
import concourse.mybir as mybir
from concourse.bass_utils import run_bass_kernel_spmd

F32 = mybir.dt.float32
BF16 = mybir.dt.bfloat16
AF = mybir.ActivationFunctionType
ALU = mybir.AluOpType
AX = mybir.AxisListType

T = 2048
D = 1024
DI = 2048
DFF = 4096
NE = 8
NH = 8
DST = 16
DTR = 64
KD = D // 128
KI = DI // 128
NDFF = DFF // 128
TC = 512
NCH = T // TC
NTT = T // 128
LN_EPS = 1e-5
ATT_SCALE = 16.0
ATT_EPS = 1e-6
L2_EPS = 1e-12
KB = 1024
DBG_FFN_CHUNKS = None
DBG_FFN_PART = None


class Sem:
    def __init__(self, nc, name):
        self.h = nc.alloc_semaphore(name)
        self.n = 0


class Rot:
    def __init__(self, n):
        self.last = [None] * n

    def wait(self, Wfn, eng, j):
        Wfn(eng, self.last[j])

    def set(self, j, mark):
        self.last[j] = mark


def build(stage="full"):
    nc = bass.Bass("TRN2", target_bir_lowering=False, debug=False)

    def hin(name, shape, dtype=F32):
        return nc.dram_tensor(name, shape, dtype, kind="ExternalInput")

    xT_h = hin("xT", [D, T])
    wq_h = hin("wq", [D, 512])
    wkv_h = hin("wkv", [D, 1024])
    wo_h = hin("wo", [512, D])
    wr_h = hin("wr", [D, NE])
    brc_h = hin("brc", [NE, 1])
    w1_h = hin("w1", [NE, D, DFF])
    w2_h = hin("w2", [NE, D, DFF])
    w3_h = hin("w3", [NE, DFF, D])
    b1_h = hin("b1", [128, NE, NDFF])
    b2_h = hin("b2", [128, NE, NDFF])
    b3_h = hin("b3", [128, NE, KD])
    winT_h = hin("winT", [D, 2 * DI])
    wxT_h = hin("wxT", [DI, DTR + 2 * DST])
    wdtT_h = hin("wdtT", [DTR, DI])
    woutT_h = hin("woutT", [DI, D])
    convw_h = hin("convw", [128, KI, 4])
    convb_h = hin("convb", [128, KI])
    bdt_h = hin("bdt", [128, KI])
    alog_h = hin("alog", [128, KI, DST])
    dskip_h = hin("dskip", [128, KI])
    ident_h = hin("identc", [128, 128])
    ones_h = hin("onesc", [128, 128])
    sel2_h = hin("sel2c", [2, 128])
    blk2_h = hin("blk2c", [128, 2])
    sel8_h = hin("sel8c", [NE, NE * 128])
    selb_h = hin("selbc", [DST, DST * 128])

    outT_h = nc.dram_tensor("outT", [D, T], F32, kind="ExternalOutput")

    w1s = nc.dram_tensor("w1s", [NE, D, DFF], BF16, kind="Internal")
    w2s = nc.dram_tensor("w2s", [NE, D, DFF], BF16, kind="Internal")
    w3s = nc.dram_tensor("w3s", [NE, DFF, D], BF16, kind="Internal")
    winTs = nc.dram_tensor("winTs", [D, 2 * DI], BF16, kind="Internal")
    woutTs = nc.dram_tensor("woutTs", [DI, D], BF16, kind="Internal")
    wxTs = nc.dram_tensor("wxTs", [DI, DTR + 2 * DST], BF16, kind="Internal")
    wdtTs = nc.dram_tensor("wdtTs", [DTR, DI], BF16, kind="Internal")

    xT = nc.alloc_sbuf_tensor("xTs", [128, KD, T], F32)
    ident = nc.alloc_sbuf_tensor("ident", [128, 128], F32)
    ones = nc.alloc_sbuf_tensor("ones", [128, 128], F32)
    sel2 = nc.alloc_sbuf_tensor("sel2", [2, 128], F32)
    blk2 = nc.alloc_sbuf_tensor("blk2", [128, 2], F32)
    sel8 = nc.alloc_sbuf_tensor("sel8", [NE, NE * 128], F32)
    A_sb = nc.alloc_sbuf_tensor("A_sb", [128, KI, DST], F32)
    convw = nc.alloc_sbuf_tensor("convws", [128, KI, 4], F32)
    convb = nc.alloc_sbuf_tensor("convbs", [128, KI], F32)
    bdt = nc.alloc_sbuf_tensor("bdts", [128, KI], F32)
    dskip = nc.alloc_sbuf_tensor("dskips", [128, KI], F32)
    b1c = nc.alloc_sbuf_tensor("b1c", [128, NE, NDFF], F32)
    b2c = nc.alloc_sbuf_tensor("b2c", [128, NE, NDFF], F32)
    b3c = nc.alloc_sbuf_tensor("b3c", [128, NE, KD], F32)
    wrf = nc.alloc_sbuf_tensor("wrf", [128, KD, NE], F32)
    brc = nc.alloc_sbuf_tensor("brcs", [NE, 1], F32)
    brb = nc.alloc_sbuf_tensor("brb", [128, NE], F32)
    cst = nc.alloc_sbuf_tensor("cst", [128, NE], F32)
    csc = nc.alloc_sbuf_tensor("csc", [NE, 1], F32)
    csr = nc.alloc_sbuf_tensor("csr", [1, NE], F32)
    brr = nc.alloc_sbuf_tensor("brr", [1, NE], F32)
    kconst = nc.alloc_sbuf_tensor("kconst", [128, 2], F32)
    abBc = nc.alloc_sbuf_tensor("abBc", [DST, 512], F32)
    abCc = nc.alloc_sbuf_tensor("abCc", [DST, 512], F32)
    carry = nc.alloc_sbuf_tensor("carry", [128, KI, DST], F32)
    tail = nc.alloc_sbuf_tensor("tail", [128, KI, 4], F32)

    ARENA = 128 * KB
    arena = nc.alloc_sbuf_tensor("arena", [128, ARENA], mybir.dt.uint8)
    arf = arena.ap().bitcast(F32)
    arb = arena.ap().bitcast(BF16)

    def carve(off, shape, dtype):
        p = shape[0]
        n = int(np.prod(shape[1:]))
        esz = 4 if dtype == F32 else 2
        assert off % esz == 0 and off + n * esz <= ARENA, (off, shape)
        v = (arf if dtype == F32 else arb)[0:p, off // esz: off // esz + n]
        if len(shape) == 2:
            return v
        if len(shape) == 3:
            return v.rearrange("p (a b) -> p a b", a=shape[1], b=shape[2])
        return v.rearrange("p (a b c) -> p a b c", a=shape[1], b=shape[2], c=shape[3])

    aa_xn = carve(0, [128, KD, TC], BF16)
    aa_wq = carve(8 * KB, [128, KD, 512], BF16)
    aa_wkv = carve(16 * KB, [128, KD, 1024], BF16)
    aa_qraw = carve(32 * KB, [128, 512], F32)
    aa_sq = carve(34 * KB, [128, 512], F32)
    aa_sd = carve(36 * KB, [2, 512], F32)
    aa_rf = carve(38 * KB, [2, 512], F32)
    aa_pT = carve(0, [128, NTT, 512], BF16)
    aa_rs = carve(16 * KB, [1, 512], F32)
    aa_rfr = carve(18 * KB, [1, 512], F32)
    aa_rb = carve(20 * KB, [128, 512], F32)
    aa_ln_sq = [carve(32 * KB, [128, 512], F32), carve(34 * KB, [128, 512], F32)]
    aa_ln_rows = [carve(36 * KB, [1, 512], F32), carve(38 * KB, [1, 512], F32),
                  carve(113 * KB, [1, 512], F32)]
    aa_wo = carve(40 * KB, [128, 4, 1024], BF16)
    aa_qhat = carve(48 * KB, [128, 4, T], BF16)
    aa_khat = carve(64 * KB, [128, 4, T], BF16)
    aa_vaug = carve(80 * KB, [128, NTT, NH, 65], BF16)
    aa_outT = carve(97 * KB, [128, 4, T], BF16)

    cv_stg = [carve(115 * KB + j * 4 * KB, [128, 2048], BF16) for j in range(2)]

    ff_xn = carve(0, [128, KD, TC], BF16)
    ff_h = carve(8 * KB, [128, NDFF, TC], BF16)
    ff_y = carve(40 * KB, [128, KD, TC], F32)
    ff_w1 = [carve(56 * KB + j * 4 * KB, [128, KD, 256], BF16) for j in range(2)]
    ff_w2 = [carve(64 * KB + j * 4 * KB, [128, KD, 256], BF16) for j in range(2)]
    ff_w3 = [carve(72 * KB + j * 4 * KB, [128, 2, 1024], BF16) for j in range(2)]
    ff_cwb = carve(80 * KB, [128, TC], F32)
    ff_cwT = carve(82 * KB, [NE, TC], F32)
    ff_lgT = carve(84 * KB, [NE, TC], F32)
    ff_lg = carve(86 * KB, [128, NE], F32)
    ff_lgm = carve(86 * KB + 64, [128, NE], F32)
    ff_cw = carve(86 * KB + 128, [128, NE], F32)
    ff_m = carve(86 * KB + 192, [128, 8], F32)
    ff_acol = carve(86 * KB + 256, [128, 1], F32)
    ff_bcol = carve(86 * KB + 272, [128, 1], F32)
    ff_t1 = [carve(88 * KB + j * 2 * KB, [128, 512], F32) for j in range(2)]
    ff_t2 = [carve(92 * KB + j * 2 * KB, [128, 512], F32) for j in range(2)]
    ff_sl = [carve(96 * KB + j * KB, [128, 512], BF16) for j in range(2)]
    ff_b2 = [carve(98 * KB + j * KB, [128, 512], BF16) for j in range(2)]
    ff_ln_sq = [carve(100 * KB, [128, 512], F32), carve(102 * KB, [128, 512], F32)]
    ff_ln_rows = [carve(104 * KB, [1, 512], F32), carve(106 * KB, [1, 512], F32),
                  carve(108 * KB, [1, 512], F32)]

    ab_selb = carve(0, [DST, DST * 128], F32)
    ab_wxT = carve(8 * KB, [128, KI, DTR + 2 * DST], BF16)
    ab_wdtT = carve(11 * KB, [DTR, DI], BF16)
    ab_dt = carve(16 * KB, [128, KI, TC], F32)
    ab_zs = carve(48 * KB, [128, KI, TC], BF16)
    ab_y = carve(64 * KB, [128, KI, TC], F32)
    ab_xi = carve(96 * KB, [128, KI, TC], BF16)
    ab_xn = carve(112 * KB, [128, KD, TC], BF16)
    ab_wing = [carve(120 * KB + j * 4 * KB, [128, KD, 256], BF16) for j in range(2)]
    ab_xiraw = [carve(64 * KB + j * 2560, [128, 528], F32) for j in range(2)]
    ab_cv = [carve(70 * KB + j * 2 * KB, [128, 512], F32) for j in range(2)]
    ab_spl = [carve(74 * KB + j * 2 * KB, [128, 512], F32) for j in range(2)]
    ab_dtr = carve(78 * KB, [DTR, 512], BF16)

    ab_ln_sq = [carve(64 * KB, [128, 512], F32), carve(66 * KB, [128, 512], F32)]
    ab_ln_rows = [carve(74 * KB, [1, 512], F32), carve(76 * KB, [1, 512], F32),
                  carve(83 * KB, [1, 512], F32)]
    ab_Bb = carve(112 * KB, [128, TC], F32)
    ab_Cb = carve(114 * KB, [128, TC], F32)
    ab_dApre = [carve(116 * KB, [128, TC], F32)]
    ab_dA = [carve(118 * KB, [128, TC], F32)]
    ab_h = [carve(120 * KB + j * 2 * KB, [128, TC], F32) for j in range(2)]
    ab_dBx = [carve(124 * KB + j * 2 * KB, [128, TC], F32) for j in range(2)]
    ab_yz = ab_xi
    ab_wog = [carve(16 * KB + j * 8 * KB, [128, KI, 256], BF16) for j in range(2)]

    pb = [nc.alloc_psum_tensor(f"pb{i}", [128, 512], F32) for i in range(8)]
    bankf = {i: None for i in range(8)}

    sLD = Sem(nc, "sLD")
    sW12 = [Sem(nc, "sW12a"), Sem(nc, "sW12b")]
    sW3 = [Sem(nc, "sW3a"), Sem(nc, "sW3b")]
    sWing = [Sem(nc, "sWinga"), Sem(nc, "sWingb")]
    sWog = [Sem(nc, "sWoga"), Sem(nc, "sWogb")]
    sCVc = Sem(nc, "sCVc")
    sCVs = [Sem(nc, "sCVs0"), Sem(nc, "sCVs1")]
    sPE = Sem(nc, "sPE")
    sVE = Sem(nc, "sVE")
    sSE = Sem(nc, "sSE")
    sGP = Sem(nc, "sGP")
    sOUT = Sem(nc, "sOUT")

    qq = {"sy": [], "pe": [], "ve": [], "se": [], "gp": []}

    def Q(eng, fn):
        qq[eng].append(fn)

    def W(eng, mark):
        if mark is None:
            return
        sem, v = mark
        Q(eng, lambda e, s=sem, v=v: e.wait_ge(s.h, v))

    def DR(eng):
        Q(eng, lambda e: e.drain())

    def OP(eng, sem, builder, k=1):
        if sem is None:
            Q(eng, builder)
            return None
        sem.n += k
        v = sem.n
        Q(eng, lambda e, b=builder, s=sem, kk=k: b(e).then_inc(s.h, kk))
        return (sem, v)

    def bank_wait(eng, *banks):
        for i in banks:
            W(eng, bankf[i])

    ld_xt = OP("sy", sLD, lambda e: e.dma_start(
        out=xT[:, :, :], in_=xT_h.ap().rearrange("(k p) t -> p k t", p=128)), 16)
    m_pro = ld_xt
    for dst, src in [
        (ident[:, :], ident_h[:, :]), (ones[:, :], ones_h[:, :]),
        (sel2[:, :], sel2_h[:, :]), (blk2[:, :], blk2_h[:, :]),
        (sel8[:, :], sel8_h[:, :]), (A_sb[:, :, :], alog_h[:, :, :]),
        (convw[:, :, :], convw_h[:, :, :]), (convb[:, :], convb_h[:, :]),
        (bdt[:, :], bdt_h[:, :]), (dskip[:, :], dskip_h[:, :]),
        (b1c[:, :, :], b1_h[:, :, :]), (b2c[:, :, :], b2_h[:, :, :]),
        (b3c[:, :, :], b3_h[:, :, :]),
        (wrf[:, :, :], wr_h.ap().rearrange("(k p) e -> p k e", p=128)),
        (brc[:, :], brc_h[:, :]),
    ]:
        m_pro = OP("sy", sLD, lambda e, d=dst, s=src: e.dma_start(out=d, in_=s), 16)

    for dst, src in [
        (aa_wq[:, :, :], wq_h.ap().rearrange("(k p) m -> p k m", p=128)),
        (aa_wkv[:, :, :], wkv_h.ap().rearrange("(k p) m -> p k m", p=128)),
        (aa_wo[:, :, :], wo_h.ap().rearrange("(k p) m -> p k m", p=128)),
    ]:
        m_c = OP("gp", sCVc, lambda e, d=dst, s=src: e.dma_start(out=d, in_=s), 16)
    W("gp", m_c)
    m_aa_w = OP("gp", sGP, lambda e: e.nop())
    m_vaug1 = OP("gp", sGP, lambda e: e.memset(aa_vaug[:, :, :, :], 1.0))
    DR("gp")
    OP("gp", None, lambda e: e.memset(kconst[:, 0:1], -ATT_SCALE))
    DR("gp")
    m_kc = OP("gp", sGP, lambda e: e.memset(kconst[:, 1:2], LN_EPS))
    DR("gp")
    m_tailz = OP("gp", sGP, lambda e: e.memset(tail[:, :, :], 0.0))
    DR("gp")
    m_carryz = OP("gp", sGP, lambda e: e.memset(carry[:, :, :], 0.0))

    W("se", m_pro)
    m_aexp = OP("se", sSE, lambda e: e.activation(A_sb[:, :, :], A_sb[:, :, :], AF.Exp,
                                                  bias=0.0, scale=1.0))
    W("ve", m_aexp)
    m_aneg = OP("ve", sVE, lambda e: e.tensor_scalar_mul(A_sb[:, :, :], A_sb[:, :, :], -1.0))

    W("pe", m_pro)
    for k in range(KD):
        m_cs = OP("pe", sPE if k == KD - 1 else None,
                  lambda e, k=k: e.matmul(pb[0][0:NE, 0:1], wrf[:, k, :], ones[:, 0:1],
                                          start=(k == 0), stop=(k == KD - 1)))
    W("ve", m_cs)
    m_csc = OP("ve", sVE, lambda e: e.tensor_copy(csc[:, :], pb[0][0:NE, 0:1]))
    bankf[0] = m_csc
    W("pe", m_csc)
    m_trc = OP("pe", sPE, lambda e: e.transpose(pb[1][0:1, 0:NE], csc[:, :], ident[0:NE, 0:NE]))
    m_trb = OP("pe", sPE, lambda e: e.transpose(pb[2][0:1, 0:NE], brc[:, :], ident[0:NE, 0:NE]))
    W("ve", m_trb)
    m_csr = OP("ve", sVE, lambda e: e.tensor_copy(csr[:, :], pb[1][0:1, 0:NE]))
    DR("ve")
    m_brr = OP("ve", sVE, lambda e: e.tensor_copy(brr[:, :], pb[2][0:1, 0:NE]))
    bankf[1] = m_csr
    bankf[2] = m_brr
    W("pe", m_brr)
    m_bc1 = OP("pe", sPE, lambda e: e.matmul(pb[3][:, 0:NE], ones[0:1, :], csr[:, :],
                                             start=True, stop=True))
    m_bc2 = OP("pe", sPE, lambda e: e.matmul(pb[4][:, 0:NE], ones[0:1, :], brr[:, :],
                                             start=True, stop=True))
    W("ve", m_bc2)
    m_cst = OP("ve", sVE, lambda e: e.tensor_copy(cst[:, :], pb[3][:, 0:NE]))
    DR("ve")
    m_brb = OP("ve", sVE, lambda e: e.tensor_copy(brb[:, :], pb[4][:, 0:NE]))
    bankf[3] = m_cst
    bankf[4] = m_brb

    conv_jobs = []

    def add_jobs(src2d, dst2d, rows, cols, tag):
        for k in range((rows + 127) // 128):
            r0, r1 = k * 128, min((k + 1) * 128, rows)
            for c0_ in range(0, cols, 2048):
                c1_ = min(c0_ + 2048, cols)
                conv_jobs.append((src2d[r0:r1, c0_:c1_], dst2d[r0:r1, c0_:c1_],
                                  c1_ - c0_, tag))

    for e_ in range(NE):
        add_jobs(w1_h[e_], w1s[e_], D, DFF, ("e", e_))
        add_jobs(w2_h[e_], w2s[e_], D, DFF, ("e", e_))
        add_jobs(w3_h[e_], w3s[e_], DFF, D, ("e", e_))
    add_jobs(wxT_h, wxTs, DI, DTR + 2 * DST, ("wx",))
    add_jobs(wdtT_h, wdtTs, DTR, DI, ("wdt",))
    add_jobs(winT_h, winTs, D, 2 * DI, ("win",))
    add_jobs(woutT_h, woutTs, DI, D, ("wout",))
    conv_done = {}
    last_store = [None, None]
    for j, (src, dst, cols, tag) in enumerate(conv_jobs):
        b = j % 2
        if j >= 2:
            W("gp", last_store[b])
        rows_ = src.shape[0]
        m_c = OP("gp", sCVc, lambda e, s=src, d=cv_stg[b][0:rows_, 0:cols]: e.dma_start(out=d, in_=s), 16)
        W("gp", m_c)
        m_bar = OP("gp", sGP, lambda e: e.nop())
        W("sy", m_bar)
        m_s = OP("sy", sCVs[b], lambda e, s=cv_stg[b][0:rows_, 0:cols], d=dst: e.dma_start(out=d, in_=s), 16)
        last_store[b] = m_s
        conv_done[tag] = (last_store[0], last_store[1])
    m_conv_all = conv_done[("wout",)]

    def emit_ln(c, xn_ap, sq, rows, pw_pe=(), pw_se=(), pw_ve=()):
        c0 = c * TC
        r_mean, r_var, r_tmp = rows
        for w_ in pw_se:
            W("se", w_)
        for w_ in pw_pe:
            W("pe", w_)
        bank_wait("pe", 4, 5)
        mm_m = []
        for k in range(KD):
            if k >= 2:
                W("se", mm_m[k - 2])
            m_sq = OP("se", sSE, lambda e, k=k: e.square(sq[k % 2][:, :], xT[:, k, c0:c0 + TC]))
            W("pe", m_sq)
            OP("pe", None, lambda e, k=k: e.matmul(pb[4][0:1, :], ones[:, 0:1],
                                                   xT[:, k, c0:c0 + TC],
                                                   start=(k == 0), stop=(k == KD - 1)))
            m = OP("pe", sPE, lambda e, k=k: e.matmul(pb[5][0:1, :], ones[:, 0:1],
                                                      sq[k % 2][:, :],
                                                      start=(k == 0), stop=(k == KD - 1)))
            mm_m.append(m)
        m_stats = m
        for w_ in pw_ve:
            W("ve", w_)
        W("ve", m_stats)
        OP("ve", None, lambda e: e.tensor_scalar_mul(r_mean[:, :], pb[4][0:1, :], 1.0 / D))
        OP("ve", None, lambda e: e.tensor_scalar_mul(r_var[:, :], pb[5][0:1, :], 1.0 / D))
        DR("ve")
        OP("ve", None, lambda e: e.tensor_tensor(r_tmp[:, :], r_mean[:, :], r_mean[:, :], ALU.mult))
        DR("ve")
        m_var = OP("ve", sVE, lambda e: e.tensor_tensor(r_var[:, :], r_var[:, :], r_tmp[:, :],
                                                        ALU.subtract))
        bankf[4] = m_var
        bankf[5] = m_var
        W("se", m_var)
        m_sd = OP("se", sSE, lambda e: e.activation(r_tmp[:, :], r_var[:, :], AF.Sqrt,
                                                    bias=kconst[0:1, 1:2], scale=1.0))
        W("ve", m_sd)
        OP("ve", None, lambda e: e.reciprocal(r_var[:, :], r_tmp[:, :]))
        DR("ve")
        m_rows = OP("ve", sVE, lambda e: e.scalar_tensor_tensor(
            r_tmp[:, :], r_mean[:, :], -1.0, r_var[:, :], ALU.mult, ALU.mult))
        W("pe", m_rows)
        bank_wait("pe", 6, 7)
        OP("pe", None, lambda e: e.matmul(pb[6][:, :], ones[0:1, :], r_var[:, :],
                                          start=True, stop=True))
        m_bc = OP("pe", sPE, lambda e: e.matmul(pb[7][:, :], ones[0:1, :], r_tmp[:, :],
                                                start=True, stop=True))
        W("ve", m_bc)
        for k in range(KD):
            OP("ve", None, lambda e, k=k: e.tensor_tensor(sq[k % 2][:, :],
                                                          xT[:, k, c0:c0 + TC],
                                                          pb[6][:, :], ALU.mult))
            DR("ve")
            m_xn = OP("ve", sVE, lambda e, k=k: e.tensor_tensor(xn_ap[:, k, :],
                                                                sq[k % 2][:, :],
                                                                pb[7][:, :], ALU.add))
        DR("ve")
        bankf[6] = m_xn
        bankf[7] = m_xn
        return m_xn, r_var, r_tmp, m_rows

    m_res = ld_xt
    if stage in ("attn_a", "ffn1", "attn_b", "full"):
        W("pe", m_pro)
        W("se", m_pro)
        W("se", (sGP, m_kc[1]))
        m_hat_last = None
        m_v_last = None
        m_ss_last = None
        vb = Rot(2)
        for c in range(NCH):
            pw_se = [m_hat_last] if m_hat_last else []
            m_xn, _, _, _ = emit_ln(c, aa_xn, aa_ln_sq, aa_ln_rows,
                                    pw_pe=([m_aa_w] if c == 0 else []), pw_se=pw_se)
            for which in ("q", "k"):
                for mq in range(4):
                    W("pe", m_xn)
                    bank_wait("pe", 0)
                    for k in range(KD):
                        lhs = (aa_wq if which == "q" else aa_wkv)[:, k, mq * 128:(mq + 1) * 128]
                        m_pj = OP("pe", sPE if k == KD - 1 else None,
                                  lambda e, l=lhs, k=k: e.matmul(
                                      pb[0][:, :], l, aa_xn[:, k, :],
                                      start=(k == 0), stop=(k == KD - 1)))
                    W("ve", m_pj)
                    m_qraw = OP("ve", sVE, lambda e: e.tensor_copy(aa_qraw[:, :], pb[0][:, :]))
                    bankf[0] = m_qraw
                    W("se", m_qraw)
                    m_sq = OP("se", sSE, lambda e: e.square(aa_sq[:, :], aa_qraw[:, :]))
                    W("pe", m_sq)
                    bank_wait("pe", 1)
                    m_ss = OP("pe", sPE, lambda e: e.matmul(pb[1][0:2, :], blk2[:, :],
                                                            aa_sq[:, :], start=True, stop=True))
                    m_ss_last = m_ss
                    W("se", m_ss)
                    m_sd = OP("se", sSE, lambda e: e.activation(aa_sd[:, :], pb[1][0:2, :],
                                                                AF.Sqrt, bias=0.0, scale=1.0))
                    bankf[1] = m_sd
                    W("ve", m_sd)
                    OP("ve", None, lambda e: e.tensor_scalar_max(aa_sd[:, :], aa_sd[:, :], L2_EPS))
                    DR("ve")
                    m_rf = OP("ve", sVE, lambda e: e.reciprocal(aa_rf[:, :], aa_sd[:, :]))
                    W("pe", m_rf)
                    bank_wait("pe", 2)
                    m_rfb = OP("pe", sPE, lambda e: e.matmul(pb[2][:, :], sel2[:, :],
                                                             aa_rf[:, :], start=True, stop=True))
                    W("ve", m_rfb)
                    dst = (aa_qhat if which == "q" else aa_khat)[:, mq, c * TC:(c + 1) * TC]
                    m_hat = OP("ve", sVE, lambda e, d=dst: e.tensor_tensor(
                        d, aa_qraw[:, :], pb[2][:, :], ALU.mult))
                    DR("ve")
                    bankf[2] = m_hat
                    m_hat_last = m_hat
            for tt in range(c * 4, c * 4 + 4):
                bank = 3 if tt % 2 == 0 else 0
                W("pe", m_xn)
                if tt == 0:
                    W("pe", (sGP, m_vaug1[1]))
                bank_wait("pe", bank)
                for k in range(KD):
                    m_vmm = OP("pe", sPE if k == KD - 1 else None,
                               lambda e, k=k, tt=tt, b=bank: e.matmul(
                                   pb[b][:, :],
                                   aa_xn[:, k, (tt % 4) * 128:((tt % 4) + 1) * 128],
                                   aa_wkv[:, k, 512:1024],
                                   start=(k == 0), stop=(k == KD - 1)))
                W("ve", m_vmm)
                for h in range(NH):
                    m_vc = OP("ve", sVE if h == NH - 1 else None,
                              lambda e, h=h, tt=tt, b=bank: e.tensor_copy(
                                  aa_vaug[:, tt, h, 0:64], pb[b][:, h * 64:(h + 1) * 64]))
                DR("ve")
                bankf[bank] = m_vc
                m_v_last = m_vc

        pvdone = {}
        expm = {}
        for c in range(NCH):
            for h in range(NH):
                idx = c * NH + h
                r0 = (h % 2) * 64
                W("pe", m_hat_last)
                if idx == 0:
                    W("pe", m_v_last)
                else:
                    W("pe", pvdone[idx - 1])
                if idx == 0:
                    W("se", m_ss_last)
                else:
                    W("se", pvdone[idx - 1])
                for kt in range(NTT):
                    if kt >= 2:
                        W("pe", expm[(idx, kt - 2)])
                    m_qk = OP("pe", sPE, lambda e, kt=kt, h=h, c=c, r0=r0: e.matmul(
                        pb[4 + (kt % 2)][:, :],
                        aa_khat[r0:r0 + 64, h // 2, kt * 128:(kt + 1) * 128],
                        aa_qhat[r0:r0 + 64, h // 2, c * TC:(c + 1) * TC],
                        start=True, stop=True))
                    W("se", m_qk)
                    expm[(idx, kt)] = OP("se", sSE, lambda e, kt=kt: e.activation(
                        aa_pT[:, kt, :], pb[4 + (kt % 2)][:, :], AF.Exp,
                        bias=kconst[:, 0:1], scale=ATT_SCALE))
                for kt in range(NTT):
                    W("pe", expm[(idx, kt)])
                    OP("pe", None, lambda e, kt=kt, h=h, r0=r0: e.matmul(
                        pb[6][r0:r0 + 64, :], aa_vaug[:, kt, h, 0:64], aa_pT[:, kt, :],
                        start=(kt == 0), stop=(kt == NTT - 1)))
                    m_pv = OP("pe", sPE if kt == NTT - 1 else None,
                              lambda e, kt=kt, h=h: e.matmul(
                                  pb[7][0:1, :], aa_vaug[:, kt, h, 64:65], aa_pT[:, kt, :],
                                  start=(kt == 0), stop=(kt == NTT - 1)))
                W("ve", m_pv)
                OP("ve", None, lambda e: e.tensor_scalar_max(aa_rs[:, :], pb[7][0:1, :], ATT_EPS))
                DR("ve")
                m_rcp = OP("ve", sVE, lambda e: e.reciprocal(aa_rfr[:, :], aa_rs[:, :]))
                W("pe", m_rcp)
                bank_wait("pe", 3)
                m_rcb = OP("pe", sPE, lambda e, r0=r0: e.matmul(
                    pb[3][r0:r0 + 64, :], ones[0:1, 0:64], aa_rfr[:, :], start=True, stop=True))
                W("ve", m_rcb)
                OP("ve", None, lambda e, r0=r0: e.tensor_copy(aa_rb[r0:r0 + 64, :],
                                                              pb[3][r0:r0 + 64, :]))
                DR("ve")
                m_eo = OP("ve", sVE, lambda e, r0=r0, h=h, c=c: e.tensor_tensor(
                    aa_outT[r0:r0 + 64, h // 2, c * TC:(c + 1) * TC],
                    pb[6][r0:r0 + 64, :], aa_rb[r0:r0 + 64, :], ALU.mult))
                DR("ve")
                pvdone[idx] = m_eo
                bankf[3] = m_eo
        for i in range(8):
            bankf[i] = pvdone[NCH * NH - 1]

        for c in range(NCH):
            for md in range(KD):
                bank = md % 2
                W("pe", pvdone[c * NH + NH - 1])
                bank_wait("pe", bank)
                for ki in range(4):
                    m_wo = OP("pe", sPE if ki == 3 else None,
                              lambda e, ki=ki, md=md, c=c, b=bank: e.matmul(
                                  pb[b][:, :], aa_wo[:, ki, md * 128:(md + 1) * 128],
                                  aa_outT[:, ki, c * TC:(c + 1) * TC],
                                  start=(ki == 0), stop=(ki == 3)))
                W("ve", m_wo)
                m_add = OP("ve", sVE, lambda e, md=md, c=c, b=bank: e.tensor_tensor(
                    xT[:, md, c * TC:(c + 1) * TC], pb[b][:, :],
                    xT[:, md, c * TC:(c + 1) * TC], ALU.add))
                DR("ve")
                bankf[bank] = m_add
        m_res = m_add

    ffn_state = {"w12": Rot(2), "w3": Rot(2), "sl": Rot(2), "t1": Rot(2),
                 "h_last2": [None, None]}

    def emit_ffn(pidx, m_prev):
        rw12, rw3 = ffn_state["w12"], ffn_state["w3"]
        rsl, rt1 = ffn_state["sl"], ffn_state["t1"]
        for j in (0, 1):
            rw12.last[j] = m_prev
            rw3.last[j] = m_prev
        m_last_vadd = m_prev
        for c in range(DBG_FFN_CHUNKS or NCH):
            c0 = c * TC
            m_xn, r_rstd, r_brow, m_rows = emit_ln(
                c, ff_xn, ff_ln_sq, ff_ln_rows,
                pw_pe=[m_prev], pw_se=[m_prev, m_last_vadd], pw_ve=[m_last_vadd])
            if c > 0 and DBG_FFN_PART == "ln":
                break
            bank_wait("pe", 0)
            W("pe", m_prev)
            for k in range(KD):
                m_lg = OP("pe", sPE if k == KD - 1 else None,
                          lambda e, k=k, c0=c0: e.matmul(pb[0][0:NE, :], wrf[:, k, :],
                                                         xT[:, k, c0:c0 + TC],
                                                         start=(k == 0), stop=(k == KD - 1)))
            W("ve", m_lg)
            m_lgT = OP("ve", sVE, lambda e: e.tensor_copy(ff_lgT[:, :], pb[0][0:NE, :]))
            DR("ve")
            bankf[0] = m_lgT
            m_cT = None
            for tt in range(4):
                W("pe", m_lgT)
                W("pe", m_rows)
                bank_wait("pe", 1, 2, 3)
                OP("pe", None, lambda e, tt=tt: e.transpose(
                    pb[1][0:128, 0:NE], ff_lgT[0:NE, tt * 128:(tt + 1) * 128],
                    ident[0:NE, 0:NE]))
                OP("pe", None, lambda e, tt=tt, r=r_rstd: e.matmul(
                    pb[2][:, 0:1], r[:, tt * 128:(tt + 1) * 128], ones[0:1, 0:1],
                    start=True, stop=True))
                m_t3 = OP("pe", sPE, lambda e, tt=tt, r=r_brow: e.matmul(
                    pb[3][:, 0:1], r[:, tt * 128:(tt + 1) * 128], ones[0:1, 0:1],
                    start=True, stop=True))
                W("ve", m_t3)
                OP("ve", None, lambda e: e.tensor_copy(ff_lg[:, :], pb[1][0:128, 0:NE]))
                OP("ve", None, lambda e: e.tensor_copy(ff_acol[:, :], pb[2][:, 0:1]))
                m_cp = OP("ve", sVE, lambda e: e.tensor_copy(ff_bcol[:, :], pb[3][:, 0:1]))
                bankf[1] = m_cp
                bankf[2] = m_cp
                bankf[3] = m_cp
                DR("ve")
                OP("ve", None, lambda e: e.scalar_tensor_tensor(
                    ff_lg[:, :], ff_lg[:, :], ff_acol[:, 0:1], brb[:, :], ALU.mult, ALU.add))
                DR("ve")
                OP("ve", None, lambda e: e.scalar_tensor_tensor(
                    ff_lg[:, :], cst[:, :], ff_bcol[:, 0:1], ff_lg[:, :], ALU.mult, ALU.add))
                DR("ve")
                OP("ve", None, lambda e: e.reduce_max(ff_m[:, 0:1], ff_lg[:, :], axis=AX.X))
                DR("ve")
                OP("ve", None, lambda e: e.tensor_scalar(ff_cw[:, :], ff_lg[:, :],
                                                         ff_m[:, 0:1], None, ALU.is_ge))
                DR("ve")
                OP("ve", None, lambda e: e.scalar_tensor_tensor(
                    ff_lgm[:, :], ff_cw[:, :], -1e30, ff_lg[:, :], ALU.mult, ALU.add))
                DR("ve")
                m_m2 = OP("ve", sVE, lambda e: e.reduce_max(ff_m[:, 1:2], ff_lgm[:, :], axis=AX.X))
                W("se", m_m2)
                OP("se", None, lambda e: e.activation(ff_m[:, 2:3], ff_m[:, 1:2], AF.Copy,
                                                      bias=0.0, scale=-1.0))
                DR("se")
                OP("se", None, lambda e: e.activation(ff_m[:, 3:4], ff_m[:, 0:1], AF.Sigmoid,
                                                      bias=ff_m[:, 2:3], scale=1.0))
                DR("se")
                m_w2s = OP("se", sSE, lambda e: e.activation(ff_m[:, 4:5], ff_m[:, 3:4], AF.Copy,
                                                             bias=1.0, scale=-1.0))
                W("ve", m_w2s)
                OP("ve", None, lambda e: e.tensor_scalar(ff_lgm[:, :], ff_lgm[:, :],
                                                         ff_m[:, 1:2], None, ALU.is_ge))
                OP("ve", None, lambda e: e.tensor_scalar(ff_cw[:, :], ff_cw[:, :],
                                                         ff_m[:, 3:4], None, ALU.mult))
                DR("ve")
                m_cw = OP("ve", sVE, lambda e: e.scalar_tensor_tensor(
                    ff_cw[:, :], ff_lgm[:, :], ff_m[:, 4:5], ff_cw[:, :], ALU.mult, ALU.add))
                DR("ve")
                W("pe", m_cw)
                bank_wait("pe", 1)
                m_ctr = OP("pe", sPE, lambda e: e.transpose(pb[1][0:NE, 0:128], ff_cw[:, :],
                                                            ident[:, :]))
                W("ve", m_ctr)
                m_cT = OP("ve", sVE, lambda e, tt=tt: e.tensor_copy(
                    ff_cwT[:, tt * 128:(tt + 1) * 128], pb[1][0:NE, 0:128]))
                DR("ve")
                bankf[1] = m_cT
            m_router = m_cT
            if c > 0 and DBG_FFN_PART == "router":
                break

            for e_ in range(NE):
                if e_ == 0:
                    W("pe", m_router)
                h_marks = []
                for g in range(16):
                    j = g % 2
                    if g == 8:
                        bank_wait("pe", 4)
                        m_cwbmm = OP("pe", sPE, lambda e, e_=e_: e.matmul(
                            pb[4][:, :], sel8[:, e_ * 128:(e_ + 1) * 128], ff_cwT[0:NE, :],
                            start=True, stop=True))
                        W("ve", m_cwbmm)
                        m_cwb = OP("ve", sVE, lambda e: e.tensor_copy(ff_cwb[:, :], pb[4][:, :]))
                        DR("ve")
                        bankf[4] = m_cwb
                    rw12.wait(W, "sy", j)
                    if pidx == 1 and c == 0:
                        for m_ in conv_done[("e", e_)]:
                            W("sy", m_)
                    OP("sy", sW12[j], lambda e, e_=e_, g=g, j=j: e.dma_start(
                        out=ff_w1[j][:, :, :],
                        in_=w1s[e_, :, g * 256:(g + 1) * 256].rearrange(
                            "(k p) m -> p k m", p=128)), 16)
                    m_l = OP("sy", sW12[j], lambda e, e_=e_, g=g, j=j: e.dma_start(
                        out=ff_w2[j][:, :, :],
                        in_=w2s[e_, :, g * 256:(g + 1) * 256].rearrange(
                            "(k p) m -> p k m", p=128)), 16)
                    for dt2 in range(2):
                        dff = g * 2 + dt2
                        bb = 2 * dt2
                        W("pe", m_l)
                        W("pe", m_xn)
                        if len(h_marks) >= 2:
                            W("pe", h_marks[-2])
                        else:
                            bank_wait("pe", bb, bb + 1)
                        for k in range(KD):
                            OP("pe", None, lambda e, k=k, j=j, dt2=dt2, b=bb: e.matmul(
                                pb[b][:, :], ff_w1[j][:, k, dt2 * 128:(dt2 + 1) * 128],
                                ff_xn[:, k, :], start=(k == 0), stop=(k == KD - 1)))
                        for k in range(KD):
                            m_mm = OP("pe", sPE if k == KD - 1 else None,
                                      lambda e, k=k, j=j, dt2=dt2, b=bb: e.matmul(
                                          pb[b + 1][:, :],
                                          ff_w2[j][:, k, dt2 * 128:(dt2 + 1) * 128],
                                          ff_xn[:, k, :], start=(k == 0), stop=(k == KD - 1)))
                        W("se", m_mm)
                        rsl.wait(W, "se", dff % 2)
                        m_sl = OP("se", sSE, lambda e, e_=e_, dff=dff, b=bb: e.activation(
                            ff_sl[dff % 2][:, :], pb[b][:, :], AF.Silu,
                            bias=b1c[:, e_, dff:dff + 1], scale=1.0))
                        W("ve", m_mm)
                        W("ve", m_sl)
                        m_h = OP("ve", sVE, lambda e, e_=e_, dff=dff, b=bb: e.scalar_tensor_tensor(
                            ff_h[:, dff, :], pb[b + 1][:, :], b2c[:, e_, dff:dff + 1],
                            ff_sl[dff % 2][:, :], ALU.add, ALU.mult))
                        DR("ve")
                        h_marks.append(m_h)
                        rsl.set(dff % 2, m_h)
                        if dt2 == 1:
                            rw12.set(j, m_mm)
                W("pe", h_marks[-1])
                W("pe", h_marks[-2])
                bank_wait("pe", 5, 6, 7)
                W("pe", m_cwb)
                for gg in range(16):
                    j = gg % 2
                    rw3.wait(W, "sy", j)
                    m_wl = OP("sy", sW3[j], lambda e, e_=e_, gg=gg, j=j: e.dma_start(
                        out=ff_w3[j][:, :, :],
                        in_=w3s[e_, gg * 256:(gg + 1) * 256, :].rearrange(
                            "(k p) m -> p k m", p=128)), 16)
                    W("pe", m_wl)
                    W("pe", h_marks[gg * 2 + 1])
                    for md in range(KD):
                        for jj in range(2):
                            m_w3 = OP("pe",
                                      sPE if (md == KD - 1 and jj == 1) else None,
                                      lambda e, gg=gg, j=j, jj=jj, md=md: e.matmul(
                                          pb[md][:, :],
                                          ff_w3[j][:, jj, md * 128:(md + 1) * 128],
                                          ff_h[:, gg * 2 + jj, :],
                                          start=(gg == 0 and jj == 0),
                                          stop=(gg == 15 and jj == 1)))
                    rw3.set(j, m_w3)
                for md in range(KD):
                    W("se", m_w3)
                    rt1.wait(W, "se", md % 2)
                    m_t1e = OP("se", sSE, lambda e, e_=e_, md=md: e.activation(
                        ff_t1[md % 2][:, :], pb[md][:, :], AF.Identity,
                        bias=b3c[:, e_, md:md + 1], scale=1.0))
                    bankf[md] = m_t1e
                    if e_ == 0:
                        W("ve", m_t1e)
                        W("ve", m_cwb)
                        m_ya = OP("ve", sVE, lambda e, md=md: e.tensor_tensor(
                            ff_y[:, md, :], ff_t1[md % 2][:, :], ff_cwb[:, :], ALU.mult))
                        DR("ve")
                    else:
                        W("gp", m_t1e)
                        W("gp", m_cwb)
                        rt1.wait(W, "gp", md % 2)
                        m_t2 = OP("gp", sGP, lambda e, md=md: e.tensor_tensor(
                            ff_t2[md % 2][:, :], ff_t1[md % 2][:, :], ff_cwb[:, :], ALU.mult))
                        W("ve", m_t2)
                        m_ya = OP("ve", sVE, lambda e, md=md: e.tensor_tensor(
                            ff_y[:, md, :], ff_y[:, md, :], ff_t2[md % 2][:, :], ALU.add))
                        DR("ve")
                    rt1.set(md % 2, m_ya)
            for md in range(KD):
                m_vadd = OP("ve", sVE, lambda e, md=md, c0=c0: e.tensor_tensor(
                    xT[:, md, c0:c0 + TC], ff_y[:, md, :], xT[:, md, c0:c0 + TC], ALU.add))
                DR("ve")
            m_last_vadd = m_vadd
        return m_last_vadd

    if stage in ("ffn1", "attn_b", "full"):
        m_res = emit_ffn(1, m_res)

    if stage in ("attn_b", "full"):
        for m_ in m_conv_all:
            W("sy", m_)
        W("sy", m_res)
        m_selb = OP("sy", sLD, lambda e: e.dma_start(out=ab_selb[:, :], in_=selb_h[:, :]), 16)
        OP("sy", sLD, lambda e: e.dma_start(
            out=ab_wxT[:, :, :], in_=wxTs.ap().rearrange("(k p) m -> p k m", p=128)), 16)
        m_wx = OP("sy", sLD, lambda e: e.dma_start(out=ab_wdtT[:, :], in_=wdtTs[:, :]), 16)
        m_wdt = m_wx
        m_selb = m_wx

        g_exp = {}
        g_scan = {}
        g_scan_it = {}
        g_yadd = {}
        g_dbx = {}
        g_hc = {}
        git = [0]
        rwing = Rot(2)
        rwog = Rot(2)
        for j in (0, 1):
            rwing.last[j] = m_res
            rwog.last[j] = m_res
        scan_rel = [None, None]
        m_res_ve = m_res
        for c in range(NCH):
            c0 = c * TC
            m_xn, _, _, _ = emit_ln(c, ab_xn, ab_ln_sq, ab_ln_rows,
                                    pw_pe=[m_res], pw_se=[m_res, m_res_ve],
                                    pw_ve=[m_conv_all[0], m_conv_all[1], m_res_ve])
            xz_m = {}
            for g in range(16):
                j = g % 2
                rwing.wait(W, "sy", j)
                if g == 0:
                    W("sy", scan_rel[0])
                    W("sy", scan_rel[1])
                if c == 0 and g == 0:
                    for m_ in conv_done[("win",)]:
                        W("sy", m_)
                m_l = OP("sy", sWing[j], lambda e, g=g, j=j: e.dma_start(
                    out=ab_wing[j][:, :, :],
                    in_=winTs[:, g * 256:(g + 1) * 256].rearrange("(k p) m -> p k m", p=128)), 16)
                for m2 in range(2):
                    mi = g * 2 + m2
                    bb = mi % 2
                    W("pe", m_l)
                    W("pe", m_xn)
                    bank_wait("pe", bb)
                    for k in range(KD):
                        m_mm = OP("pe", sPE if k == KD - 1 else None,
                                  lambda e, k=k, j=j, m2=m2, b=bb: e.matmul(
                                      pb[b][:, :], ab_wing[j][:, k, m2 * 128:(m2 + 1) * 128],
                                      ab_xn[:, k, :], start=(k == 0), stop=(k == KD - 1)))
                    if m2 == 1:
                        rwing.set(j, m_mm)
                    if mi < KI:
                        W("ve", m_mm)
                        if c == 0 and mi == 0:
                            W("ve", (sGP, m_tailz[1]))
                        xr = ab_xiraw[mi % 2]
                        OP("ve", None, lambda e, mi=mi, xr=xr: e.tensor_copy(
                            xr[:, 0:3], tail[:, mi, 0:3]))
                        DR("ve")
                        m_xr = OP("ve", sVE, lambda e, xr=xr, b=bb: e.tensor_copy(
                            xr[:, 3:515], pb[b][:, :]))
                        bankf[bb] = m_xr
                        DR("ve")
                        OP("ve", None, lambda e, mi=mi, xr=xr: e.tensor_copy(
                            tail[:, mi, 0:3], xr[:, 512:515]))
                        DR("ve")
                        cv = ab_cv[mi % 2]
                        if mi >= 2:
                            W("ve", xz_m[mi - 2])
                        OP("ve", None, lambda e, mi=mi, xr=xr, cv=cv: e.tensor_scalar(
                            cv[:, :], xr[:, 0:512], convw[:, mi, 0:1], None, ALU.mult))
                        DR("ve")
                        for jj in range(1, 3):
                            OP("ve", None, lambda e, mi=mi, xr=xr, cv=cv, jj=jj:
                               e.scalar_tensor_tensor(cv[:, :], xr[:, jj:jj + 512],
                                                      convw[:, mi, jj:jj + 1], cv[:, :],
                                                      ALU.mult, ALU.add))
                            DR("ve")
                        m_cv = OP("ve", sVE, lambda e, mi=mi, xr=xr, cv=cv:
                                  e.scalar_tensor_tensor(cv[:, :], xr[:, 3:515][:, 0:512],
                                                         convw[:, mi, 3:4], cv[:, :],
                                                         ALU.mult, ALU.add))
                        DR("ve")
                        W("se", m_cv)
                        m_xi = OP("se", sSE, lambda e, mi=mi, cv=cv: e.activation(
                            ab_xi[:, mi, :], cv[:, :], AF.Silu,
                            bias=convb[:, mi:mi + 1], scale=1.0))
                        xz_m[mi] = m_xi
                    else:
                        W("se", m_mm)
                        m_z = OP("se", sSE, lambda e, mi=mi, b=bb: e.activation(
                            ab_zs[:, mi - KI, :], pb[b][:, :], AF.Silu, bias=0.0, scale=1.0))
                        bankf[bb] = m_z
                        xz_m[mi] = m_z
            m_z_all = xz_m[2 * KI - 1]

            if c == 0:
                W("pe", m_wx)
            bank_wait("pe", 2, 3, 4)
            for mi in range(KI):
                W("pe", xz_m[mi])
                OP("pe", None, lambda e, mi=mi: e.matmul(
                    pb[2][0:DTR, :], ab_wxT[:, mi, 0:DTR], ab_xi[:, mi, :],
                    start=(mi == 0), stop=(mi == KI - 1)))
                OP("pe", None, lambda e, mi=mi: e.matmul(
                    pb[3][0:DST, :], ab_wxT[:, mi, DTR:DTR + DST], ab_xi[:, mi, :],
                    start=(mi == 0), stop=(mi == KI - 1)))
                m_xd = OP("pe", sPE if mi == KI - 1 else None,
                          lambda e, mi=mi: e.matmul(
                              pb[4][0:DST, :], ab_wxT[:, mi, DTR + DST:DTR + 2 * DST],
                              ab_xi[:, mi, :], start=(mi == 0), stop=(mi == KI - 1)))
            W("ve", m_xd)
            m_dtr = OP("ve", sVE, lambda e: e.tensor_copy(ab_dtr[:, :], pb[2][0:DTR, :]))
            DR("ve")
            bankf[2] = m_dtr
            W("se", m_xd)
            OP("se", None, lambda e: e.copy(abBc[:, :], pb[3][0:DST, :]))
            DR("se")
            m_BC = OP("se", sSE, lambda e: e.copy(abCc[:, :], pb[4][0:DST, :]))
            bankf[3] = m_BC
            bankf[4] = m_BC

            if c == 0:
                W("pe", m_wdt)
            W("pe", m_dtr)
            dt_m = []
            for mi in range(KI):
                bb = 5 + mi % 2
                bank_wait("pe", bb)
                m_dp = OP("pe", sPE, lambda e, mi=mi, b=bb: e.matmul(
                    pb[b][:, :], ab_wdtT[:, mi * 128:(mi + 1) * 128], ab_dtr[:, :],
                    start=True, stop=True))
                W("ve", m_dp)
                v = ab_spl[mi % 2]
                av = ab_cv[mi % 2]
                m_v0 = OP("ve", sVE, lambda e, mi=mi, v=v, b=bb: e.tensor_scalar(
                    v[:, :], pb[b][:, :], bdt[:, mi:mi + 1], None, ALU.add))
                bankf[bb] = m_v0
                DR("ve")
                OP("ve", None, lambda e, v=v, av=av: e.tensor_scalar(
                    av[:, :], v[:, :], -1.0, None, ALU.mult))
                DR("ve")
                m_av = OP("ve", sVE, lambda e, v=v, av=av: e.tensor_tensor(
                    av[:, :], v[:, :], av[:, :], ALU.max))
                W("se", m_av)
                OP("se", None, lambda e, av=av: e.activation(av[:, :], av[:, :], AF.Exp,
                                                             bias=0.0, scale=-1.0))
                DR("se")
                m_l1p = OP("se", sSE, lambda e, av=av: e.activation(av[:, :], av[:, :], AF.Ln,
                                                                   bias=1.0, scale=1.0))
                DR("ve")
                OP("ve", None, lambda e, v=v: e.tensor_scalar_max(v[:, :], v[:, :], 0.0))
                DR("ve")
                W("ve", m_l1p)
                m_dt = OP("ve", sVE, lambda e, mi=mi, v=v, av=av: e.tensor_tensor(
                    ab_dt[:, mi, :], v[:, :], av[:, :], ALU.add))
                DR("ve")
                dt_m.append(m_dt)
            m_dt_all = dt_m[-1]

            m_u = None
            W("ve", (sGP, sGP.n))
            for mi in range(KI):
                W("ve", dt_m[mi])
                OP("ve", None, lambda e, mi=mi: e.tensor_scalar(
                    ab_y[:, mi, :], ab_xi[:, mi, :], dskip[:, mi:mi + 1], None, ALU.mult))
                DR("ve")
                m_u = OP("ve", sVE, lambda e, mi=mi: e.tensor_tensor(
                    ab_xi[:, mi, :], ab_dt[:, mi, :], ab_xi[:, mi, :], ALU.mult))
                DR("ve")
            m_u_all = m_u
            ab_u = ab_xi

            if c == 0:
                W("ve", (sGP, m_carryz[1]))
                W("ve", m_aneg)
                W("se", m_aneg)
            W("se", m_dt_all)

            def emit_tail(itp):
                mip = itp % KI
                jbp = itp % 2
                g_hc[itp] = OP("ve", sVE, lambda e, jbp=jbp: e.tensor_tensor(
                    ab_dBx[jbp][:, :], ab_h[jbp][:, :], ab_Cb[:, :], ALU.mult))
                DR("ve")
                g_yadd[itp] = OP("ve", sVE, lambda e, mip=mip, jbp=jbp: e.tensor_tensor(
                    ab_y[:, mip, :], ab_y[:, mip, :], ab_dBx[jbp][:, :], ALU.add))
                DR("ve")

            for n in range(DST):
                it0 = git[0] * KI
                W("pe", m_BC)
                if c == 0 and n == 0:
                    W("pe", m_selb)
                bank_wait("pe", 7, 2)
                m_bbm = OP("pe", sPE, lambda e, n=n: e.matmul(
                    pb[7][:, :], ab_selb[:, n * 128:(n + 1) * 128], abBc[:, :],
                    start=True, stop=True))
                m_cbm = OP("pe", sPE, lambda e, n=n: e.matmul(
                    pb[2][:, :], ab_selb[:, n * 128:(n + 1) * 128], abCc[:, :],
                    start=True, stop=True))
                W("se", m_cbm)
                if it0 >= 1:
                    W("se", g_hc[it0 - 1])
                OP("se", None, lambda e: e.copy(ab_Bb[:, :], pb[7][:, :]))
                DR("se")
                m_bcb = OP("se", sSE, lambda e: e.copy(ab_Cb[:, :], pb[2][:, :]))
                bankf[7] = m_bcb
                bankf[2] = m_bcb
                W("gp", m_bcb)
                if n == 0:
                    W("gp", m_u_all)
                for mi0 in range(2):
                    itb = it0 + mi0
                    if itb >= 2:
                        W("gp", g_yadd[itb - 2])
                    g_dbx[itb] = OP("gp", sGP, lambda e, mi0=mi0, jb=itb % 2, itb=itb: e.tensor_tensor(
                        ab_dBx[jb][:, :], ab_u[:, (itb % KI), :], ab_Bb[:, :], ALU.mult))
                for mi in range(KI):
                    it = it0 + mi
                    jb = it % 2
                    if it >= 1:
                        W("se", g_scan_it[it - 1])
                        OP("se", None, lambda e, it=it: e.copy(
                            carry[:, (it - 1) % KI, (it - 1) // KI % DST:
                                  (it - 1) // KI % DST + 1],
                            ab_h[(it - 1) % 2][:, 511:512]))
                        DR("se")
                    g_exp[it] = OP("se", sSE, lambda e, mi=mi, n=n: e.activation(
                        ab_dA[0][:, :], ab_dt[:, mi, :], AF.Exp, bias=0.0,
                        scale=A_sb[:, mi, n:n + 1]))
                    if 1 <= mi <= KI - 2:
                        W("gp", g_yadd[it - 1])
                        g_dbx[it + 1] = OP("gp", sGP, lambda e, mi=mi, jb=(it + 1) % 2:
                                           e.tensor_tensor(ab_dBx[jb][:, :],
                                                           ab_u[:, mi + 1, :],
                                                           ab_Bb[:, :], ALU.mult))
                    if it == 0 and c == 0:
                        W("ve", m_u_all)
                    W("ve", g_exp[it])
                    W("ve", g_dbx[it])
                    m_sc = OP("ve", sVE, lambda e, mi=mi, n=n, jb=jb: e.tensor_tensor_scan(
                        ab_h[jb][:, :], ab_dA[0][:, :], ab_dBx[jb][:, :],
                        carry[:, mi, n:n + 1], ALU.mult, ALU.add))
                    g_scan_it[it] = m_sc
                    DR("ve")
                    emit_tail(it)
                git[0] += 1
            m_y = g_yadd[git[0] * KI - 1]
            itf = git[0] * KI - 1
            W("se", g_scan_it[itf])
            m_carryf = OP("se", sSE, lambda e, itf=itf: e.copy(
                carry[:, itf % KI, (itf // KI) % DST:(itf // KI) % DST + 1],
                ab_h[itf % 2][:, 511:512]))
            scan_rel = [m_carryf, m_y]

            m_yz = None
            W("ve", m_z_all)
            for mi in range(KI):
                m_yz = OP("ve", sVE, lambda e, mi=mi: e.tensor_tensor(
                    ab_yz[:, mi, :], ab_y[:, mi, :], ab_zs[:, mi, :], ALU.mult))
                DR("ve")
            for g2 in range(4):
                j = g2 % 2
                rwog.wait(W, "sy", j)
                W("sy", m_yz)
                if c == 0 and g2 == 0:
                    for m_ in conv_done[("wout",)]:
                        W("sy", m_)
                m_wl = OP("sy", sWog[j], lambda e, g2=g2, j=j: e.dma_start(
                    out=ab_wog[j][:, :, :],
                    in_=woutTs[:, g2 * 256:(g2 + 1) * 256].rearrange(
                        "(k p) m -> p k m", p=128)), 16)
                for m2 in range(2):
                    md = g2 * 2 + m2
                    bb = 3 + md % 2
                    W("pe", m_wl)
                    W("pe", m_yz)
                    bank_wait("pe", bb)
                    for kk in range(KI):
                        m_mm = OP("pe", sPE if kk == KI - 1 else None,
                                  lambda e, kk=kk, j=j, m2=m2, b=bb: e.matmul(
                                      pb[b][:, :], ab_wog[j][:, kk, m2 * 128:(m2 + 1) * 128],
                                      ab_yz[:, kk, :], start=(kk == 0), stop=(kk == KI - 1)))
                    if m2 == 1:
                        rwog.set(j, m_mm)
                    W("ve", m_mm)
                    m_vadd = OP("ve", sVE, lambda e, md=md, c0=c0, b=bb: e.tensor_tensor(
                        xT[:, md, c0:c0 + TC], pb[b][:, :], xT[:, md, c0:c0 + TC], ALU.add))
                    DR("ve")
                    bankf[bb] = m_vadd
            m_res_ve = m_vadd
        m_res = m_res_ve

    if stage == "full":
        m_res = emit_ffn(2, m_res)

    W("sy", (sVE, sVE.n))
    W("sy", (sSE, sSE.n))
    W("sy", (sPE, sPE.n))
    W("sy", (sGP, sGP.n))
    OP("sy", sOUT, lambda e: e.dma_start(
        out=outT_h.ap().rearrange("(k p) t -> p k t", p=128), in_=xT[:, :, :]), 16)
    W("sy", (sOUT, 16))

    with nc.Block() as block:
        block.sync(lambda e: [f(e) for f in qq["sy"]])
        block.tensor(lambda e: [f(e) for f in qq["pe"]])
        block.vector(lambda e: [f(e) for f in qq["ve"]])
        block.scalar(lambda e: [f(e) for f in qq["se"]])
        block.gpsimd(lambda e: [f(e) for f in qq["gp"]])
    return nc


def _prep_shared(i):
    f32 = np.float32
    sel8 = np.zeros((NE, NE * 128), f32)
    for e in range(NE):
        sel8[e, e * 128:(e + 1) * 128] = 1.0
    selb = np.zeros((DST, DST * 128), f32)
    for n in range(DST):
        selb[n, n * 128:(n + 1) * 128] = 1.0
    sel2 = np.zeros((2, 128), f32)
    sel2[0, 0:64] = 1.0
    sel2[1, 64:128] = 1.0
    blk2 = np.zeros((128, 2), f32)
    blk2[0:64, 0] = 1.0
    blk2[64:128, 1] = 1.0
    c = np.ascontiguousarray
    return {
        "wq": c(np.asarray(i["L0_attn_a_Wq"], f32)),
        "wkv": c(np.asarray(i["L0_attn_a_Wkv"], f32)),
        "wo": c(np.asarray(i["L0_attn_a_Wo"], f32)),
        "wr": c(np.asarray(i["L0_ffn_Wr"], f32)),
        "brc": c(np.asarray(i["L0_ffn_br"], f32).reshape(NE, 1)),
        "w1": c(np.asarray(i["L0_ffn_W1"], f32)),
        "w2": c(np.asarray(i["L0_ffn_W2"], f32)),
        "w3": c(np.asarray(i["L0_ffn_W3"], f32)),
        "b1": c(np.asarray(i["L0_ffn_b1"], f32).reshape(NE, NDFF, 128).transpose(2, 0, 1)),
        "b2": c(np.asarray(i["L0_ffn_b2"], f32).reshape(NE, NDFF, 128).transpose(2, 0, 1)),
        "b3": c(np.asarray(i["L0_ffn_b3"], f32).reshape(NE, KD, 128).transpose(2, 0, 1)),
        "winT": c(np.asarray(i["L0_attn_b_W_in"], f32).T),
        "wxT": c(np.asarray(i["L0_attn_b_W_x"], f32).T),
        "wdtT": c(np.asarray(i["L0_attn_b_W_dt"], f32).T),
        "woutT": c(np.asarray(i["L0_attn_b_W_out"], f32).T),
        "convw": c(np.asarray(i["L0_attn_b_conv_w"], f32)[:, 0, :]
                   .reshape(KI, 128, 4).transpose(1, 0, 2)),
        "convb": c(np.asarray(i["L0_attn_b_conv_b"], f32).reshape(KI, 128).T),
        "bdt": c(np.asarray(i["L0_attn_b_b_dt"], f32).reshape(KI, 128).T),
        "alog": c(np.asarray(i["L0_attn_b_A_log"], f32).reshape(KI, 128, DST)
                  .transpose(1, 0, 2)),
        "dskip": c(np.asarray(i["L0_attn_b_D_skip"], f32).reshape(KI, 128).T),
        "identc": np.eye(128, dtype=f32),
        "onesc": np.ones((128, 128), f32),
        "sel2c": sel2,
        "blk2c": blk2,
        "sel8c": sel8,
        "selbc": selb,
    }


def kernel(**inputs):
    x = np.asarray(inputs["L0_attn_a_x"], np.float32)
    B = x.shape[0]
    shared = _prep_shared(inputs)
    nc = build("full")
    in_maps = []
    for b in range(B):
        m = dict(shared)
        m["xT"] = np.ascontiguousarray(x[b].T)
        in_maps.append(m)
    res = run_bass_kernel_spmd(nc, in_maps, core_ids=list(range(B)))
    out = np.stack([np.asarray(r["outT"]).T for r in res.results], axis=0)
    return out.astype(np.float32)


# version 10 (final)
# speedup vs baseline: 1.1083x; 1.1007x over previous
import numpy as np
import ml_dtypes
import concourse.bass as bass
import concourse.mybir as mybir
from concourse.bass_utils import run_bass_kernel_spmd

F32 = mybir.dt.float32
BF16 = mybir.dt.bfloat16
AF = mybir.ActivationFunctionType
ALU = mybir.AluOpType
AX = mybir.AxisListType

T = 2048
D = 1024
DI = 2048
DFF = 4096
NE = 8
NH = 8
DST = 16
DTR = 64
KD = D // 128
KI = DI // 128
NDFF = DFF // 128
TC = 512
NCH = T // TC
NTT = T // 128
LN_EPS = 1e-5
ATT_SCALE = 16.0
ATT_EPS = 1e-6
L2_EPS = 1e-12
KB = 1024
DBG_FFN_CHUNKS = None
DBG_FFN_PART = None


class Sem:
    def __init__(self, nc, name):
        self.h = nc.alloc_semaphore(name)
        self.n = 0


class Rot:
    def __init__(self, n):
        self.last = [None] * n

    def wait(self, Wfn, eng, j):
        Wfn(eng, self.last[j])

    def set(self, j, mark):
        self.last[j] = mark


def build(stage="full"):
    nc = bass.Bass("TRN2", target_bir_lowering=False, debug=False)

    def hin(name, shape, dtype=F32):
        return nc.dram_tensor(name, shape, dtype, kind="ExternalInput")

    xT_h = hin("xT", [D, T])
    wq_h = hin("wq", [D, 512])
    wkv_h = hin("wkv", [D, 1024])
    wo_h = hin("wo", [512, D])
    wr_h = hin("wr", [D, NE])
    brc_h = hin("brc", [NE, 1])
    w1_h = hin("w1", [NE, D, DFF])
    w2_h = hin("w2", [NE, D, DFF])
    w3_h = hin("w3", [NE, DFF, D])
    b1_h = hin("b1", [128, NE, NDFF])
    b2_h = hin("b2", [128, NE, NDFF])
    b3r_h = hin("b3r", [NE, D])
    ffc_h = hin("ffc", [128, 260])
    identb_h = hin("identbc", [128, 128], BF16)
    winT_h = hin("winT", [D, 2 * DI])
    wxT_h = hin("wxT", [DI, DTR + 2 * DST])
    wdtT_h = hin("wdtT", [DTR, DI])
    woutT_h = hin("woutT", [DI, D])
    convw_h = hin("convw", [128, KI, 4])
    convb_h = hin("convb", [128, KI])
    bdt_h = hin("bdt", [128, KI])
    alog_h = hin("alog", [128, KI, DST])
    dskip_h = hin("dskip", [128, KI])
    ident_h = hin("identc", [128, 128])
    ones_h = hin("onesc", [128, 128])
    sel2_h = hin("sel2c", [2, 128])
    blk2_h = hin("blk2c", [128, 2])
    selb_h = hin("selbc", [DST, DST * 128])

    outT_h = nc.dram_tensor("outT", [D, T], F32, kind="ExternalOutput")

    w1s = nc.dram_tensor("w1s", [NE, D, DFF], BF16, kind="Internal")
    w2s = nc.dram_tensor("w2s", [NE, D, DFF], BF16, kind="Internal")
    w3s = nc.dram_tensor("w3s", [NE, DFF, D], BF16, kind="Internal")
    winTs = nc.dram_tensor("winTs", [D, 2 * DI], BF16, kind="Internal")
    woutTs = nc.dram_tensor("woutTs", [DI, D], BF16, kind="Internal")
    wxTs = nc.dram_tensor("wxTs", [DI, DTR + 2 * DST], BF16, kind="Internal")
    wdtTs = nc.dram_tensor("wdtTs", [DTR, DI], BF16, kind="Internal")

    xT = nc.alloc_sbuf_tensor("xTs", [128, KD, T], F32)
    ident = nc.alloc_sbuf_tensor("ident", [128, 128], F32)
    ones = nc.alloc_sbuf_tensor("ones", [128, 128], F32)
    sel2 = nc.alloc_sbuf_tensor("sel2", [2, 128], F32)
    blk2 = nc.alloc_sbuf_tensor("blk2", [128, 2], F32)
    identb = nc.alloc_sbuf_tensor("identb", [128, 128], BF16)
    A_sb = nc.alloc_sbuf_tensor("A_sb", [128, KI, DST], F32)
    convw = nc.alloc_sbuf_tensor("convws", [128, KI, 4], F32)
    convb = nc.alloc_sbuf_tensor("convbs", [128, KI], F32)
    bdt = nc.alloc_sbuf_tensor("bdts", [128, KI], F32)
    dskip = nc.alloc_sbuf_tensor("dskips", [128, KI], F32)
    b1c = nc.alloc_sbuf_tensor("b1c", [128, NE, NDFF], F32)
    b2c = nc.alloc_sbuf_tensor("b2c", [128, NE, NDFF], F32)
    b3r = nc.alloc_sbuf_tensor("b3r_sb", [NE, D], F32)
    wrf = nc.alloc_sbuf_tensor("wrf", [128, KD, NE], F32)
    brc = nc.alloc_sbuf_tensor("brcs", [NE, 1], F32)
    brb = nc.alloc_sbuf_tensor("brb", [128, NE], F32)
    cst = nc.alloc_sbuf_tensor("cst", [128, NE], F32)
    csc = nc.alloc_sbuf_tensor("csc", [NE, 1], F32)
    csr = nc.alloc_sbuf_tensor("csr", [1, NE], F32)
    brr = nc.alloc_sbuf_tensor("brr", [1, NE], F32)
    kconst = nc.alloc_sbuf_tensor("kconst", [128, 2], F32)
    abBc = nc.alloc_sbuf_tensor("abBc", [DST, 512], F32)
    abCc = nc.alloc_sbuf_tensor("abCc", [DST, 512], F32)
    carry = nc.alloc_sbuf_tensor("carry", [128, KI, DST], F32)
    tail = nc.alloc_sbuf_tensor("tail", [128, KI, 4], F32)

    ARENA = 128 * KB
    arena = nc.alloc_sbuf_tensor("arena", [128, ARENA], mybir.dt.uint8)
    arf = arena.ap().bitcast(F32)
    arb = arena.ap().bitcast(BF16)

    def carve(off, shape, dtype):
        p = shape[0]
        n = int(np.prod(shape[1:]))
        esz = 4 if dtype == F32 else 2
        assert off % esz == 0 and off + n * esz <= ARENA, (off, shape)
        v = (arf if dtype == F32 else arb)[0:p, off // esz: off // esz + n]
        if len(shape) == 2:
            return v
        if len(shape) == 3:
            return v.rearrange("p (a b) -> p a b", a=shape[1], b=shape[2])
        return v.rearrange("p (a b c) -> p a b c", a=shape[1], b=shape[2], c=shape[3])

    aa_xn = carve(0, [128, KD, TC], BF16)
    aa_wq = carve(8 * KB, [128, KD, 512], BF16)
    aa_wkv = carve(16 * KB, [128, KD, 1024], BF16)
    aa_qraw = carve(32 * KB, [128, 512], F32)
    aa_sq = carve(34 * KB, [128, 512], F32)
    aa_sd = carve(36 * KB, [2, 512], F32)
    aa_rf = carve(38 * KB, [2, 512], F32)
    aa_pT = carve(0, [128, NTT, 512], BF16)
    aa_rs = carve(16 * KB, [1, 512], F32)
    aa_rfr = carve(18 * KB, [1, 512], F32)
    aa_rb = carve(20 * KB, [128, 512], F32)
    aa_ln_sq = [carve(32 * KB, [128, 512], F32), carve(34 * KB, [128, 512], F32)]
    aa_ln_rows = [carve(36 * KB, [1, 512], F32), carve(38 * KB, [1, 512], F32),
                  carve(113 * KB, [1, 512], F32)]
    aa_wo = carve(40 * KB, [128, 4, 1024], BF16)
    aa_qhat = carve(48 * KB, [128, 4, T], BF16)
    aa_khat = carve(64 * KB, [128, 4, T], BF16)
    aa_vaug = carve(80 * KB, [128, NTT, NH, 65], BF16)
    aa_outT = carve(97 * KB, [128, 4, T], BF16)

    cv_stg = [carve(115 * KB + j * 4 * KB, [128, 2048], BF16) for j in range(2)]

    ff_xn = carve(0, [128, KD, TC], BF16)
    ff_xtok = carve(8 * KB, [128, 4, D], BF16)
    ff_h = carve(16 * KB, [128, NDFF, 256], BF16)
    ff_ytok = carve(32 * KB, [128, 4, D], F32)
    ff_ywe = carve(48 * KB, [128, 2, 2, 512], BF16)
    ff_P = carve(52 * KB, [128, 4, 256], BF16)
    ff_PT = carve(54 * KB, [128, 2, 512], BF16)
    ff_w1 = [carve(56 * KB + j * 4 * KB, [128, KD, 256], BF16) for j in range(2)]
    ff_w2 = [carve(64 * KB + j * 4 * KB, [128, KD, 256], BF16) for j in range(2)]
    ff_w3 = [carve(72 * KB + j * 4 * KB, [128, 2, 1024], BF16) for j in range(2)]
    ff_srA = carve(80 * KB, [128, 512], F32)
    ff_srB = carve(82 * KB, [128, 512], F32)
    ff_m2 = carve(84 * KB, [8, 1024], F32)
    ff_m2v = carve(84 * KB, [8, 512, 2], F32)
    ff_px = carve(88 * KB, [8, 1024], F32)
    ff_pxv = carve(88 * KB, [8, 512, 2], F32)
    ff_ones8 = carve(92 * KB, [8, 1024], F32)
    ff_ta = carve(96 * KB, [128, 512], F32)
    ff_tb = carve(98 * KB, [128, 512], F32)
    ff_sl = [carve(96 * KB + j * 512, [128, 256], BF16) for j in range(2)]
    ff_ln_sq = [carve(96 * KB, [128, 512], F32), carve(98 * KB, [128, 512], F32)]
    ff_ln_rows = [carve(100 * KB, [1, 512], F32), carve(102 * KB, [1, 512], F32),
                  carve(104 * KB, [1, 512], F32)]
    ff_rwA = carve(102 * KB, [1, 512], F32)
    ff_rwB = carve(104 * KB, [1, 512], F32)
    ff_lgT = carve(106 * KB, [NE, TC], F32)
    ff_cwT = carve(108 * KB, [NE, TC], F32)
    ff_xne = carve(110 * KB, [128, KD, 256], BF16)
    ff_lg = carve(114 * KB, [128, NE], F32)
    ff_lgm = carve(114 * KB + 64, [128, NE], F32)
    ff_cw = carve(114 * KB + 128, [128, NE], F32)
    ff_m = carve(114 * KB + 192, [128, 8], F32)
    ff_scol = carve(114 * KB + 256, [128, 8], F32)
    ff_acol = carve(114 * KB + 288, [128, 1], F32)
    ff_bcol = carve(114 * KB + 304, [128, 1], F32)
    ff_is1 = carve(114 * KB + 512, [128, 4, NE], F32)
    ff_is2 = carve(114 * KB + 640, [128, 4, NE], F32)
    ff_cwt = carve(114 * KB + 768, [128, 4, NE], F32)
    ff_ffc = carve(123 * KB, [128, 260], F32)

    ab_selb = carve(0, [DST, DST * 128], F32)
    ab_wxT = carve(8 * KB, [128, KI, DTR + 2 * DST], BF16)
    ab_wdtT = carve(11 * KB, [DTR, DI], BF16)
    ab_dt = carve(16 * KB, [128, KI, TC], F32)
    ab_zs = carve(48 * KB, [128, KI, TC], BF16)
    ab_y = carve(64 * KB, [128, KI, TC], F32)
    ab_xi = carve(96 * KB, [128, KI, TC], BF16)
    ab_xn = carve(112 * KB, [128, KD, TC], BF16)
    ab_wing = [carve(120 * KB + j * 4 * KB, [128, KD, 256], BF16) for j in range(2)]
    ab_xiraw = [carve(64 * KB + j * 2560, [128, 528], F32) for j in range(2)]
    ab_cv = [carve(70 * KB + j * 2 * KB, [128, 512], F32) for j in range(2)]
    ab_spl = [carve(74 * KB + j * 2 * KB, [128, 512], F32) for j in range(2)]
    ab_dtr = carve(78 * KB, [DTR, 512], BF16)

    ab_ln_sq = [carve(64 * KB, [128, 512], F32), carve(66 * KB, [128, 512], F32)]
    ab_ln_rows = [carve(74 * KB, [1, 512], F32), carve(76 * KB, [1, 512], F32),
                  carve(83 * KB, [1, 512], F32)]
    ab_Bb = carve(112 * KB, [128, TC], F32)
    ab_Cb = carve(114 * KB, [128, TC], F32)
    ab_dApre = [carve(116 * KB, [128, TC], F32)]
    ab_dA = [carve(118 * KB, [128, TC], F32)]
    ab_h = [carve(120 * KB + j * 2 * KB, [128, TC], F32) for j in range(2)]
    ab_dBx = [carve(124 * KB + j * 2 * KB, [128, TC], F32) for j in range(2)]
    ab_yz = ab_xi
    ab_wog = [carve(16 * KB + j * 8 * KB, [128, KI, 256], BF16) for j in range(2)]

    pb = [nc.alloc_psum_tensor(f"pb{i}", [128, 512], F32) for i in range(8)]
    bankf = {i: None for i in range(8)}

    sLD = Sem(nc, "sLD")
    sW12 = [Sem(nc, "sW12a"), Sem(nc, "sW12b")]
    sW3 = [Sem(nc, "sW3a"), Sem(nc, "sW3b")]
    sWing = [Sem(nc, "sWinga"), Sem(nc, "sWingb")]
    sWog = [Sem(nc, "sWoga"), Sem(nc, "sWogb")]
    sCVc = Sem(nc, "sCVc")
    sCVs = [Sem(nc, "sCVs0"), Sem(nc, "sCVs1")]
    sPE = Sem(nc, "sPE")
    sVE = Sem(nc, "sVE")
    sSE = Sem(nc, "sSE")
    sGP = Sem(nc, "sGP")
    sOUT = Sem(nc, "sOUT")

    qq = {"sy": [], "pe": [], "ve": [], "se": [], "gp": []}

    def Q(eng, fn):
        qq[eng].append(fn)

    def W(eng, mark):
        if mark is None:
            return
        sem, v = mark
        Q(eng, lambda e, s=sem, v=v: e.wait_ge(s.h, v))

    def DR(eng):
        Q(eng, lambda e: e.drain())

    def OP(eng, sem, builder, k=1):
        if sem is None:
            Q(eng, builder)
            return None
        sem.n += k
        v = sem.n
        Q(eng, lambda e, b=builder, s=sem, kk=k: b(e).then_inc(s.h, kk))
        return (sem, v)

    def bank_wait(eng, *banks):
        for i in banks:
            W(eng, bankf[i])

    ld_xt = OP("sy", sLD, lambda e: e.dma_start(
        out=xT[:, :, :], in_=xT_h.ap().rearrange("(k p) t -> p k t", p=128)), 16)
    m_pro = ld_xt
    for dst, src in [
        (ident[:, :], ident_h[:, :]), (ones[:, :], ones_h[:, :]),
        (sel2[:, :], sel2_h[:, :]), (blk2[:, :], blk2_h[:, :]),
        (identb[:, :], identb_h[:, :]), (A_sb[:, :, :], alog_h[:, :, :]),
        (convw[:, :, :], convw_h[:, :, :]), (convb[:, :], convb_h[:, :]),
        (bdt[:, :], bdt_h[:, :]), (dskip[:, :], dskip_h[:, :]),
        (b1c[:, :, :], b1_h[:, :, :]), (b2c[:, :, :], b2_h[:, :, :]),
        (b3r[:, :], b3r_h[:, :]),
        (wrf[:, :, :], wr_h.ap().rearrange("(k p) e -> p k e", p=128)),
        (brc[:, :], brc_h[:, :]),
    ]:
        m_pro = OP("sy", sLD, lambda e, d=dst, s=src: e.dma_start(out=d, in_=s), 16)

    for dst, src in [
        (aa_wq[:, :, :], wq_h.ap().rearrange("(k p) m -> p k m", p=128)),
        (aa_wkv[:, :, :], wkv_h.ap().rearrange("(k p) m -> p k m", p=128)),
        (aa_wo[:, :, :], wo_h.ap().rearrange("(k p) m -> p k m", p=128)),
    ]:
        m_c = OP("gp", sCVc, lambda e, d=dst, s=src: e.dma_start(out=d, in_=s), 16)
    W("gp", m_c)
    m_aa_w = OP("gp", sGP, lambda e: e.nop())
    m_vaug1 = OP("gp", sGP, lambda e: e.memset(aa_vaug[:, :, :, :], 1.0))
    DR("gp")
    OP("gp", None, lambda e: e.memset(kconst[:, 0:1], -ATT_SCALE))
    DR("gp")
    m_kc = OP("gp", sGP, lambda e: e.memset(kconst[:, 1:2], LN_EPS))
    DR("gp")
    m_tailz = OP("gp", sGP, lambda e: e.memset(tail[:, :, :], 0.0))
    DR("gp")
    m_carryz = OP("gp", sGP, lambda e: e.memset(carry[:, :, :], 0.0))

    W("se", m_pro)
    m_aexp = OP("se", sSE, lambda e: e.activation(A_sb[:, :, :], A_sb[:, :, :], AF.Exp,
                                                  bias=0.0, scale=1.0))
    W("ve", m_aexp)
    m_aneg = OP("ve", sVE, lambda e: e.tensor_scalar_mul(A_sb[:, :, :], A_sb[:, :, :], -1.0))

    W("pe", m_pro)
    for k in range(KD):
        m_cs = OP("pe", sPE if k == KD - 1 else None,
                  lambda e, k=k: e.matmul(pb[0][0:NE, 0:1], wrf[:, k, :], ones[:, 0:1],
                                          start=(k == 0), stop=(k == KD - 1)))
    W("ve", m_cs)
    m_csc = OP("ve", sVE, lambda e: e.tensor_copy(csc[:, :], pb[0][0:NE, 0:1]))
    bankf[0] = m_csc
    W("pe", m_csc)
    m_trc = OP("pe", sPE, lambda e: e.transpose(pb[1][0:1, 0:NE], csc[:, :], ident[0:NE, 0:NE]))
    m_trb = OP("pe", sPE, lambda e: e.transpose(pb[2][0:1, 0:NE], brc[:, :], ident[0:NE, 0:NE]))
    W("ve", m_trb)
    m_csr = OP("ve", sVE, lambda e: e.tensor_copy(csr[:, :], pb[1][0:1, 0:NE]))
    DR("ve")
    m_brr = OP("ve", sVE, lambda e: e.tensor_copy(brr[:, :], pb[2][0:1, 0:NE]))
    bankf[1] = m_csr
    bankf[2] = m_brr
    W("pe", m_brr)
    m_bc1 = OP("pe", sPE, lambda e: e.matmul(pb[3][:, 0:NE], ones[0:1, :], csr[:, :],
                                             start=True, stop=True))
    m_bc2 = OP("pe", sPE, lambda e: e.matmul(pb[4][:, 0:NE], ones[0:1, :], brr[:, :],
                                             start=True, stop=True))
    W("ve", m_bc2)
    m_cst = OP("ve", sVE, lambda e: e.tensor_copy(cst[:, :], pb[3][:, 0:NE]))
    DR("ve")
    m_brb = OP("ve", sVE, lambda e: e.tensor_copy(brb[:, :], pb[4][:, 0:NE]))
    bankf[3] = m_cst
    bankf[4] = m_brb

    conv_jobs = []

    def add_jobs(src2d, dst2d, rows, cols, tag):
        for k in range((rows + 127) // 128):
            r0, r1 = k * 128, min((k + 1) * 128, rows)
            for c0_ in range(0, cols, 2048):
                c1_ = min(c0_ + 2048, cols)
                conv_jobs.append((src2d[r0:r1, c0_:c1_], dst2d[r0:r1, c0_:c1_],
                                  c1_ - c0_, tag))

    for e_ in range(NE):
        add_jobs(w1_h[e_], w1s[e_], D, DFF, ("e", e_))
        add_jobs(w2_h[e_], w2s[e_], D, DFF, ("e", e_))
        add_jobs(w3_h[e_], w3s[e_], DFF, D, ("e", e_))
    add_jobs(wxT_h, wxTs, DI, DTR + 2 * DST, ("wx",))
    add_jobs(wdtT_h, wdtTs, DTR, DI, ("wdt",))
    add_jobs(winT_h, winTs, D, 2 * DI, ("win",))
    add_jobs(woutT_h, woutTs, DI, D, ("wout",))
    conv_done = {}
    last_store = [None, None]
    for j, (src, dst, cols, tag) in enumerate(conv_jobs):
        b = j % 2
        if j >= 2:
            W("gp", last_store[b])
        rows_ = src.shape[0]
        m_c = OP("gp", sCVc, lambda e, s=src, d=cv_stg[b][0:rows_, 0:cols]: e.dma_start(out=d, in_=s), 16)
        W("gp", m_c)
        m_bar = OP("gp", sGP, lambda e: e.nop())
        W("sy", m_bar)
        m_s = OP("sy", sCVs[b], lambda e, s=cv_stg[b][0:rows_, 0:cols], d=dst: e.dma_start(out=d, in_=s), 16)
        last_store[b] = m_s
        conv_done[tag] = (last_store[0], last_store[1])
    m_conv_all = conv_done[("wout",)]

    def emit_ln(c, xn_ap, sq, rows, pw_pe=(), pw_se=(), pw_ve=()):
        c0 = c * TC
        r_mean, r_var, r_tmp = rows
        for w_ in pw_se:
            W("se", w_)
        for w_ in pw_pe:
            W("pe", w_)
        bank_wait("pe", 4, 5)
        mm_m = []
        for k in range(KD):
            if k >= 2:
                W("se", mm_m[k - 2])
            m_sq = OP("se", sSE, lambda e, k=k: e.square(sq[k % 2][:, :], xT[:, k, c0:c0 + TC]))
            W("pe", m_sq)
            OP("pe", None, lambda e, k=k: e.matmul(pb[4][0:1, :], ones[:, 0:1],
                                                   xT[:, k, c0:c0 + TC],
                                                   start=(k == 0), stop=(k == KD - 1)))
            m = OP("pe", sPE, lambda e, k=k: e.matmul(pb[5][0:1, :], ones[:, 0:1],
                                                      sq[k % 2][:, :],
                                                      start=(k == 0), stop=(k == KD - 1)))
            mm_m.append(m)
        m_stats = m
        for w_ in pw_ve:
            W("ve", w_)
        W("ve", m_stats)
        OP("ve", None, lambda e: e.tensor_scalar_mul(r_mean[:, :], pb[4][0:1, :], 1.0 / D))
        OP("ve", None, lambda e: e.tensor_scalar_mul(r_var[:, :], pb[5][0:1, :], 1.0 / D))
        DR("ve")
        OP("ve", None, lambda e: e.tensor_tensor(r_tmp[:, :], r_mean[:, :], r_mean[:, :], ALU.mult))
        DR("ve")
        m_var = OP("ve", sVE, lambda e: e.tensor_tensor(r_var[:, :], r_var[:, :], r_tmp[:, :],
                                                        ALU.subtract))
        bankf[4] = m_var
        bankf[5] = m_var
        W("se", m_var)
        m_sd = OP("se", sSE, lambda e: e.activation(r_tmp[:, :], r_var[:, :], AF.Sqrt,
                                                    bias=kconst[0:1, 1:2], scale=1.0))
        W("ve", m_sd)
        OP("ve", None, lambda e: e.reciprocal(r_var[:, :], r_tmp[:, :]))
        DR("ve")
        m_rows = OP("ve", sVE, lambda e: e.scalar_tensor_tensor(
            r_tmp[:, :], r_mean[:, :], -1.0, r_var[:, :], ALU.mult, ALU.mult))
        W("pe", m_rows)
        bank_wait("pe", 6, 7)
        OP("pe", None, lambda e: e.matmul(pb[6][:, :], ones[0:1, :], r_var[:, :],
                                          start=True, stop=True))
        m_bc = OP("pe", sPE, lambda e: e.matmul(pb[7][:, :], ones[0:1, :], r_tmp[:, :],
                                                start=True, stop=True))
        W("ve", m_bc)
        for k in range(KD):
            OP("ve", None, lambda e, k=k: e.tensor_tensor(sq[k % 2][:, :],
                                                          xT[:, k, c0:c0 + TC],
                                                          pb[6][:, :], ALU.mult))
            DR("ve")
            m_xn = OP("ve", sVE, lambda e, k=k: e.tensor_tensor(xn_ap[:, k, :],
                                                                sq[k % 2][:, :],
                                                                pb[7][:, :], ALU.add))
        DR("ve")
        bankf[6] = m_xn
        bankf[7] = m_xn
        return m_xn, r_var, r_tmp, m_rows

    m_res = ld_xt
    if stage in ("attn_a", "ffn1", "attn_b", "full"):
        W("pe", m_pro)
        W("se", m_pro)
        W("se", (sGP, m_kc[1]))
        m_hat_last = None
        m_v_last = None
        m_ss_last = None
        vb = Rot(2)
        for c in range(NCH):
            pw_se = [m_hat_last] if m_hat_last else []
            m_xn, _, _, _ = emit_ln(c, aa_xn, aa_ln_sq, aa_ln_rows,
                                    pw_pe=([m_aa_w] if c == 0 else []), pw_se=pw_se)
            for which in ("q", "k"):
                for mq in range(4):
                    W("pe", m_xn)
                    bank_wait("pe", 0)
                    for k in range(KD):
                        lhs = (aa_wq if which == "q" else aa_wkv)[:, k, mq * 128:(mq + 1) * 128]
                        m_pj = OP("pe", sPE if k == KD - 1 else None,
                                  lambda e, l=lhs, k=k: e.matmul(
                                      pb[0][:, :], l, aa_xn[:, k, :],
                                      start=(k == 0), stop=(k == KD - 1)))
                    W("ve", m_pj)
                    m_qraw = OP("ve", sVE, lambda e: e.tensor_copy(aa_qraw[:, :], pb[0][:, :]))
                    bankf[0] = m_qraw
                    W("se", m_qraw)
                    m_sq = OP("se", sSE, lambda e: e.square(aa_sq[:, :], aa_qraw[:, :]))
                    W("pe", m_sq)
                    bank_wait("pe", 1)
                    m_ss = OP("pe", sPE, lambda e: e.matmul(pb[1][0:2, :], blk2[:, :],
                                                            aa_sq[:, :], start=True, stop=True))
                    m_ss_last = m_ss
                    W("se", m_ss)
                    m_sd = OP("se", sSE, lambda e: e.activation(aa_sd[:, :], pb[1][0:2, :],
                                                                AF.Sqrt, bias=0.0, scale=1.0))
                    bankf[1] = m_sd
                    W("ve", m_sd)
                    OP("ve", None, lambda e: e.tensor_scalar_max(aa_sd[:, :], aa_sd[:, :], L2_EPS))
                    DR("ve")
                    m_rf = OP("ve", sVE, lambda e: e.reciprocal(aa_rf[:, :], aa_sd[:, :]))
                    W("pe", m_rf)
                    bank_wait("pe", 2)
                    m_rfb = OP("pe", sPE, lambda e: e.matmul(pb[2][:, :], sel2[:, :],
                                                             aa_rf[:, :], start=True, stop=True))
                    W("ve", m_rfb)
                    dst = (aa_qhat if which == "q" else aa_khat)[:, mq, c * TC:(c + 1) * TC]
                    m_hat = OP("ve", sVE, lambda e, d=dst: e.tensor_tensor(
                        d, aa_qraw[:, :], pb[2][:, :], ALU.mult))
                    DR("ve")
                    bankf[2] = m_hat
                    m_hat_last = m_hat
            for tt in range(c * 4, c * 4 + 4):
                bank = 3 if tt % 2 == 0 else 0
                W("pe", m_xn)
                if tt == 0:
                    W("pe", (sGP, m_vaug1[1]))
                bank_wait("pe", bank)
                for k in range(KD):
                    m_vmm = OP("pe", sPE if k == KD - 1 else None,
                               lambda e, k=k, tt=tt, b=bank: e.matmul(
                                   pb[b][:, :],
                                   aa_xn[:, k, (tt % 4) * 128:((tt % 4) + 1) * 128],
                                   aa_wkv[:, k, 512:1024],
                                   start=(k == 0), stop=(k == KD - 1)))
                W("ve", m_vmm)
                for h in range(NH):
                    m_vc = OP("ve", sVE if h == NH - 1 else None,
                              lambda e, h=h, tt=tt, b=bank: e.tensor_copy(
                                  aa_vaug[:, tt, h, 0:64], pb[b][:, h * 64:(h + 1) * 64]))
                DR("ve")
                bankf[bank] = m_vc
                m_v_last = m_vc

        pvdone = {}
        expm = {}
        for c in range(NCH):
            for h in range(NH):
                idx = c * NH + h
                r0 = (h % 2) * 64
                W("pe", m_hat_last)
                if idx == 0:
                    W("pe", m_v_last)
                else:
                    W("pe", pvdone[idx - 1])
                if idx == 0:
                    W("se", m_ss_last)
                else:
                    W("se", pvdone[idx - 1])
                for kt in range(NTT):
                    if kt >= 2:
                        W("pe", expm[(idx, kt - 2)])
                    m_qk = OP("pe", sPE, lambda e, kt=kt, h=h, c=c, r0=r0: e.matmul(
                        pb[4 + (kt % 2)][:, :],
                        aa_khat[r0:r0 + 64, h // 2, kt * 128:(kt + 1) * 128],
                        aa_qhat[r0:r0 + 64, h // 2, c * TC:(c + 1) * TC],
                        start=True, stop=True))
                    W("se", m_qk)
                    expm[(idx, kt)] = OP("se", sSE, lambda e, kt=kt: e.activation(
                        aa_pT[:, kt, :], pb[4 + (kt % 2)][:, :], AF.Exp,
                        bias=kconst[:, 0:1], scale=ATT_SCALE))
                for kt in range(NTT):
                    W("pe", expm[(idx, kt)])
                    OP("pe", None, lambda e, kt=kt, h=h, r0=r0: e.matmul(
                        pb[6][r0:r0 + 64, :], aa_vaug[:, kt, h, 0:64], aa_pT[:, kt, :],
                        start=(kt == 0), stop=(kt == NTT - 1)))
                    m_pv = OP("pe", sPE if kt == NTT - 1 else None,
                              lambda e, kt=kt, h=h: e.matmul(
                                  pb[7][0:1, :], aa_vaug[:, kt, h, 64:65], aa_pT[:, kt, :],
                                  start=(kt == 0), stop=(kt == NTT - 1)))
                W("ve", m_pv)
                OP("ve", None, lambda e: e.tensor_scalar_max(aa_rs[:, :], pb[7][0:1, :], ATT_EPS))
                DR("ve")
                m_rcp = OP("ve", sVE, lambda e: e.reciprocal(aa_rfr[:, :], aa_rs[:, :]))
                W("pe", m_rcp)
                bank_wait("pe", 3)
                m_rcb = OP("pe", sPE, lambda e, r0=r0: e.matmul(
                    pb[3][r0:r0 + 64, :], ones[0:1, 0:64], aa_rfr[:, :], start=True, stop=True))
                W("ve", m_rcb)
                OP("ve", None, lambda e, r0=r0: e.tensor_copy(aa_rb[r0:r0 + 64, :],
                                                              pb[3][r0:r0 + 64, :]))
                DR("ve")
                m_eo = OP("ve", sVE, lambda e, r0=r0, h=h, c=c: e.tensor_tensor(
                    aa_outT[r0:r0 + 64, h // 2, c * TC:(c + 1) * TC],
                    pb[6][r0:r0 + 64, :], aa_rb[r0:r0 + 64, :], ALU.mult))
                DR("ve")
                pvdone[idx] = m_eo
                bankf[3] = m_eo
        for i in range(8):
            bankf[i] = pvdone[NCH * NH - 1]

        for c in range(NCH):
            for md in range(KD):
                bank = md % 2
                W("pe", pvdone[c * NH + NH - 1])
                bank_wait("pe", bank)
                for ki in range(4):
                    m_wo = OP("pe", sPE if ki == 3 else None,
                              lambda e, ki=ki, md=md, c=c, b=bank: e.matmul(
                                  pb[b][:, :], aa_wo[:, ki, md * 128:(md + 1) * 128],
                                  aa_outT[:, ki, c * TC:(c + 1) * TC],
                                  start=(ki == 0), stop=(ki == 3)))
                W("ve", m_wo)
                m_add = OP("ve", sVE, lambda e, md=md, c=c, b=bank: e.tensor_tensor(
                    xT[:, md, c * TC:(c + 1) * TC], pb[b][:, :],
                    xT[:, md, c * TC:(c + 1) * TC], ALU.add))
                DR("ve")
                bankf[bank] = m_add
        m_res = m_add

    ffn_state = {"w12": Rot(2), "w3": Rot(2), "sl": Rot(2)}

    def emit_ffn(pidx, m_prev):
        rw12, rw3 = ffn_state["w12"], ffn_state["w3"]
        rsl = ffn_state["sl"]
        for j in (0, 1):
            rw12.last[j] = m_prev
            rw3.last[j] = m_prev
            rsl.last[j] = m_prev
        m_last_vadd = m_prev
        W("sy", m_prev)
        W("sy", (sSE, sSE.n))
        W("sy", (sGP, sGP.n))
        m_ffc = OP("sy", sLD, lambda e: e.dma_start(out=ff_ffc[:, :], in_=ffc_h[:, :]), 16)
        W("ve", m_ffc)
        W("gp", m_prev)
        m_ones8 = OP("gp", sGP, lambda e: e.memset(ff_ones8[:, :], 1.0))
        m_scat_pe = m_prev
        for c in range(DBG_FFN_CHUNKS or NCH):
            c0 = c * TC
            m_xn, r_rstd, r_brow, m_rows = emit_ln(
                c, ff_xn, ff_ln_sq, ff_ln_rows,
                pw_pe=[m_prev], pw_se=[m_prev, m_last_vadd], pw_ve=[m_last_vadd])
            if c > 0 and DBG_FFN_PART == "ln":
                break
            bank_wait("pe", 0)
            W("pe", m_prev)
            for k in range(KD):
                m_lg = OP("pe", sPE if k == KD - 1 else None,
                          lambda e, k=k, c0=c0: e.matmul(pb[0][0:NE, :], wrf[:, k, :],
                                                         xT[:, k, c0:c0 + TC],
                                                         start=(k == 0), stop=(k == KD - 1)))
            W("ve", m_lg)
            m_lgT = OP("ve", sVE, lambda e: e.tensor_copy(ff_lgT[:, :], pb[0][0:NE, :]))
            DR("ve")
            bankf[0] = m_lgT
            m_cT = None
            for tt in range(4):
                W("pe", m_lgT)
                W("pe", m_rows)
                bank_wait("pe", 1, 2, 3)
                OP("pe", None, lambda e, tt=tt: e.transpose(
                    pb[1][0:128, 0:NE], ff_lgT[0:NE, tt * 128:(tt + 1) * 128],
                    ident[0:NE, 0:NE]))
                OP("pe", None, lambda e, tt=tt, r=r_rstd: e.matmul(
                    pb[2][:, 0:1], r[:, tt * 128:(tt + 1) * 128], ones[0:1, 0:1],
                    start=True, stop=True))
                m_t3 = OP("pe", sPE, lambda e, tt=tt, r=r_brow: e.matmul(
                    pb[3][:, 0:1], r[:, tt * 128:(tt + 1) * 128], ones[0:1, 0:1],
                    start=True, stop=True))
                W("ve", m_t3)
                OP("ve", None, lambda e: e.tensor_copy(ff_lg[:, :], pb[1][0:128, 0:NE]))
                OP("ve", None, lambda e: e.tensor_copy(ff_acol[:, :], pb[2][:, 0:1]))
                m_cp = OP("ve", sVE, lambda e: e.tensor_copy(ff_bcol[:, :], pb[3][:, 0:1]))
                bankf[1] = m_cp
                bankf[2] = m_cp
                bankf[3] = m_cp
                DR("ve")
                OP("ve", None, lambda e: e.scalar_tensor_tensor(
                    ff_lg[:, :], ff_lg[:, :], ff_acol[:, 0:1], brb[:, :], ALU.mult, ALU.add))
                DR("ve")
                OP("ve", None, lambda e: e.scalar_tensor_tensor(
                    ff_lg[:, :], cst[:, :], ff_bcol[:, 0:1], ff_lg[:, :], ALU.mult, ALU.add))
                DR("ve")
                OP("ve", None, lambda e: e.reduce_max(ff_m[:, 0:1], ff_lg[:, :], axis=AX.X))
                DR("ve")
                OP("ve", None, lambda e: e.tensor_scalar(ff_cw[:, :], ff_lg[:, :],
                                                         ff_m[:, 0:1], None, ALU.is_ge))
                DR("ve")
                OP("ve", None, lambda e, tt=tt: e.tensor_copy(ff_is1[:, tt, :], ff_cw[:, :]))
                OP("ve", None, lambda e: e.scalar_tensor_tensor(
                    ff_lgm[:, :], ff_cw[:, :], -1e30, ff_lg[:, :], ALU.mult, ALU.add))
                DR("ve")
                m_m2 = OP("ve", sVE, lambda e: e.reduce_max(ff_m[:, 1:2], ff_lgm[:, :], axis=AX.X))
                W("se", m_m2)
                OP("se", None, lambda e: e.activation(ff_m[:, 2:3], ff_m[:, 1:2], AF.Copy,
                                                      bias=0.0, scale=-1.0))
                DR("se")
                OP("se", None, lambda e: e.activation(ff_m[:, 3:4], ff_m[:, 0:1], AF.Sigmoid,
                                                      bias=ff_m[:, 2:3], scale=1.0))
                DR("se")
                m_w2s = OP("se", sSE, lambda e: e.activation(ff_m[:, 4:5], ff_m[:, 3:4], AF.Copy,
                                                             bias=1.0, scale=-1.0))
                W("ve", m_w2s)
                OP("ve", None, lambda e: e.tensor_scalar(ff_lgm[:, :], ff_lgm[:, :],
                                                         ff_m[:, 1:2], None, ALU.is_ge))
                OP("ve", None, lambda e: e.tensor_scalar(ff_cw[:, :], ff_cw[:, :],
                                                         ff_m[:, 3:4], None, ALU.mult))
                DR("ve")
                OP("ve", None, lambda e, tt=tt: e.tensor_copy(ff_is2[:, tt, :], ff_lgm[:, :]))
                m_cw = OP("ve", sVE, lambda e: e.scalar_tensor_tensor(
                    ff_cw[:, :], ff_lgm[:, :], ff_m[:, 4:5], ff_cw[:, :], ALU.mult, ALU.add))
                DR("ve")
                OP("ve", None, lambda e, tt=tt: e.tensor_copy(ff_cwt[:, tt, :], ff_cw[:, :]))
                DR("ve")
                W("pe", m_cw)
                bank_wait("pe", 1)
                m_ctr = OP("pe", sPE, lambda e: e.transpose(pb[1][0:NE, 0:128], ff_cw[:, :],
                                                            ident[:, :]))
                W("ve", m_ctr)
                m_cT = OP("ve", sVE, lambda e, tt=tt: e.tensor_copy(
                    ff_cwT[:, tt * 128:(tt + 1) * 128], pb[1][0:NE, 0:128]))
                DR("ve")
                bankf[1] = m_cT
            m_router = m_cT
            if c > 0 and DBG_FFN_PART == "router":
                break

            m_xtok = None
            for tt in range(4):
                bkA, bkB = (4, 5) if tt % 2 == 0 else (6, 7)
                W("pe", m_xn)
                bank_wait("pe", bkA, bkB)
                for kd in range(KD):
                    bk = bkA if kd < 4 else bkB
                    col = (kd % 4) * 128
                    m_tr = OP("pe", sPE if kd == KD - 1 else None,
                              lambda e, tt=tt, kd=kd, bk=bk, col=col: e.transpose(
                                  pb[bk].ap().bitcast(BF16)[:, col:col + 128],
                                  ff_xn[:, kd, tt * 128:(tt + 1) * 128], identb[:, :]))
                W("ve", m_tr)
                OP("ve", None, lambda e, tt=tt, bkA=bkA: e.tensor_copy(
                    ff_xtok[:, tt, 0:512], pb[bkA].ap().bitcast(BF16)[:, 0:512]))
                DR("ve")
                m_xtok = OP("ve", sVE, lambda e, tt=tt, bkB=bkB: e.tensor_copy(
                    ff_xtok[:, tt, 512:1024], pb[bkB].ap().bitcast(BF16)[:, 0:512]))
                DR("ve")
                bankf[bkA] = m_xtok
                bankf[bkB] = m_xtok

            W("pe", m_router)
            for tt in range(4):
                bk = tt % 2
                bank_wait("pe", bk)
                OP("pe", None, lambda e, tt=tt, bk=bk: e.transpose(
                    pb[bk][0:NE, 0:128], ff_is1[:, tt, :], ident[:, :]))
                m_mtr = OP("pe", sPE, lambda e, tt=tt, bk=bk: e.transpose(
                    pb[bk][0:NE, 128:256], ff_is2[:, tt, :], ident[:, :]))
                W("ve", m_mtr)
                OP("ve", None, lambda e, tt=tt, bk=bk: e.tensor_copy(
                    ff_m2v[0:NE, tt * 128:(tt + 1) * 128, 0], pb[bk][0:NE, 0:128]))
                DR("ve")
                m_m2 = OP("ve", sVE, lambda e, tt=tt, bk=bk: e.tensor_copy(
                    ff_m2v[0:NE, tt * 128:(tt + 1) * 128, 1], pb[bk][0:NE, 128:256]))
                DR("ve")
                bankf[bk] = m_m2
            if c == 0:
                W("ve", m_ones8)
            OP("ve", None, lambda e: e.tensor_tensor_scan(
                ff_px[:, :], ff_ones8[:, :], ff_m2[:, :], 0.0, ALU.mult, ALU.add))
            DR("ve")
            OP("ve", None, lambda e: e.tensor_tensor(
                ff_px[:, :], ff_px[:, :], ff_m2[:, :], ALU.subtract))
            DR("ve")
            OP("ve", None, lambda e: e.tensor_scalar_min(ff_px[:, :], ff_px[:, :], 255.0))
            DR("ve")
            OP("ve", None, lambda e: e.tensor_scalar(
                ff_ta[0:NE, :], ff_pxv[0:NE, :, 0], ff_ffc[0:NE, 257:258], None, ALU.add))
            OP("ve", None, lambda e: e.tensor_scalar(
                ff_tb[0:NE, :], ff_pxv[0:NE, :, 1], ff_ffc[0:NE, 257:258], None, ALU.add))
            DR("ve")
            OP("ve", None, lambda e: e.tensor_tensor(
                ff_ta[0:NE, :], ff_ta[0:NE, :], ff_m2v[0:NE, :, 0], ALU.mult))
            m_tab = OP("ve", sVE, lambda e: e.tensor_tensor(
                ff_tb[0:NE, :], ff_tb[0:NE, :], ff_m2v[0:NE, :, 1], ALU.mult))
            DR("ve")
            W("pe", m_tab)
            bank_wait("pe", 2, 3)
            OP("pe", None, lambda e: e.matmul(pb[2][0:1, :], ones[0:NE, 0:1],
                                              ff_ta[0:NE, :], start=True, stop=True))
            m_rmm = OP("pe", sPE, lambda e: e.matmul(pb[3][0:1, :], ones[0:NE, 0:1],
                                                     ff_tb[0:NE, :], start=True, stop=True))
            W("ve", m_rmm)
            OP("ve", None, lambda e: e.tensor_copy(ff_rwA[:, :], pb[2][0:1, :]))
            DR("ve")
            m_rws = OP("ve", sVE, lambda e: e.tensor_copy(ff_rwB[:, :], pb[3][0:1, :]))
            DR("ve")
            bankf[2] = m_rws
            bankf[3] = m_rws
            W("pe", m_rws)
            bank_wait("pe", 0, 1)
            for tt in range(4):
                for a, src in ((0, ff_rwA), (1, ff_rwB)):
                    q = 2 * tt + a
                    OP("pe", None, lambda e, tt=tt, src=src, q=q: e.transpose(
                        pb[0][0:128, q:q + 1], src[0:1, tt * 128:(tt + 1) * 128],
                        ident[0:1, 0:1]))
            OP("pe", None, lambda e: e.matmul(pb[1][:, :], ones[0:1, :], ff_rwA[:, :],
                                              start=True, stop=True))
            bank_wait("pe", 2)
            m_bc2 = OP("pe", sPE, lambda e: e.matmul(pb[2][:, :], ones[0:1, :], ff_rwB[:, :],
                                                     start=True, stop=True))
            W("ve", m_bc2)
            OP("ve", None, lambda e: e.tensor_copy(ff_scol[:, :], pb[0][0:128, 0:8]))
            OP("ve", None, lambda e: e.tensor_copy(ff_srA[:, :], pb[1][:, :]))
            m_slotv = OP("ve", sVE, lambda e: e.tensor_copy(ff_srB[:, :], pb[2][:, :]))
            DR("ve")
            bankf[0] = m_slotv
            bankf[1] = m_slotv
            bankf[2] = m_slotv

            for tc in range(4):
                for dh in range(2):
                    bk = 4 + ((tc * 2 + dh) % 2)
                    bank_wait("pe", bk)
                    m_b3 = OP("pe", sPE, lambda e, tc=tc, dh=dh, bk=bk: e.matmul(
                        pb[bk][:, :], ff_cwT[0:NE, tc * 128:(tc + 1) * 128],
                        b3r[0:NE, dh * 512:(dh + 1) * 512], start=True, stop=True))
                    W("ve", m_b3)
                    m_yini = OP("ve", sVE, lambda e, tc=tc, dh=dh, bk=bk: e.tensor_copy(
                        ff_ytok[:, tc, dh * 512:(dh + 1) * 512], pb[bk][:, :]))
                    DR("ve")
                    bankf[bk] = m_yini

            for e_ in range(NE):
                W("ve", m_scat_pe)
                m_P = None
                for tt in range(4):
                    OP("ve", None, lambda e, e_=e_, tt=tt: e.tensor_scalar(
                        ff_ta[:, 0:256], ff_ffc[:, 0:256], ff_scol[:, 2 * tt:2 * tt + 1],
                        -256.0 * e_, ALU.subtract, ALU.is_equal))
                    OP("ve", None, lambda e, e_=e_, tt=tt: e.tensor_scalar(
                        ff_tb[:, 0:256], ff_ffc[:, 0:256], ff_scol[:, 2 * tt + 1:2 * tt + 2],
                        -256.0 * e_, ALU.subtract, ALU.is_equal))
                    DR("ve")
                    m_P = OP("ve", sVE, lambda e, tt=tt: e.tensor_tensor(
                        ff_P[:, tt, :], ff_ta[:, 0:256], ff_tb[:, 0:256], ALU.add))
                    DR("ve")
                m_PT = None
                for jh in range(2):
                    OP("ve", None, lambda e, e_=e_, jh=jh: e.tensor_scalar(
                        ff_ta[:, :], ff_srA[:, :], ff_ffc[:, 256:257],
                        float(e_ * 256 + jh * 128), ALU.subtract, ALU.is_equal))
                    OP("ve", None, lambda e, e_=e_, jh=jh: e.tensor_scalar(
                        ff_tb[:, :], ff_srB[:, :], ff_ffc[:, 256:257],
                        float(e_ * 256 + jh * 128), ALU.subtract, ALU.is_equal))
                    DR("ve")
                    m_PT = OP("ve", sVE, lambda e, jh=jh: e.tensor_tensor(
                        ff_PT[:, jh, :], ff_ta[:, :], ff_tb[:, :], ALU.add))
                    DR("ve")
                W("pe", m_PT)
                if e_ == 0:
                    W("pe", m_xtok)
                for kd in range(KD):
                    bk = kd % 2
                    bank_wait("pe", bk)
                    for tt in range(4):
                        m_g = OP("pe", sPE if tt == 3 else None,
                                 lambda e, kd=kd, tt=tt, bk=bk: e.matmul(
                                     pb[bk][:, 0:256], ff_xtok[:, tt, kd * 128:(kd + 1) * 128],
                                     ff_P[:, tt, :], start=(tt == 0), stop=(tt == 3)))
                    W("se", m_g)
                    m_xe = OP("se", sSE, lambda e, kd=kd, bk=bk: e.activation(
                        ff_xne[:, kd, :], pb[bk][:, 0:256], AF.Copy, bias=0.0, scale=1.0))
                    bankf[bk] = m_xe
                m_xe_last = m_xe
                h_marks = []
                for g in range(16):
                    j = g % 2
                    rw12.wait(W, "sy", j)
                    if pidx == 1 and c == 0:
                        for m_ in conv_done[("e", e_)]:
                            W("sy", m_)
                    OP("sy", sW12[j], lambda e, e_=e_, g=g, j=j: e.dma_start(
                        out=ff_w1[j][:, :, :],
                        in_=w1s[e_, :, g * 256:(g + 1) * 256].rearrange(
                            "(k p) m -> p k m", p=128)), 16)
                    m_l = OP("sy", sW12[j], lambda e, e_=e_, g=g, j=j: e.dma_start(
                        out=ff_w2[j][:, :, :],
                        in_=w2s[e_, :, g * 256:(g + 1) * 256].rearrange(
                            "(k p) m -> p k m", p=128)), 16)
                    for dt2 in range(2):
                        dff = g * 2 + dt2
                        bb = 2 * dt2
                        W("pe", m_l)
                        if g == 0 and dt2 == 0:
                            W("pe", m_xe_last)
                        if len(h_marks) >= 2:
                            W("pe", h_marks[-2])
                        else:
                            bank_wait("pe", bb, bb + 1)
                        for k in range(KD):
                            OP("pe", None, lambda e, k=k, j=j, dt2=dt2, b=bb: e.matmul(
                                pb[b][:, 0:256], ff_w1[j][:, k, dt2 * 128:(dt2 + 1) * 128],
                                ff_xne[:, k, :], start=(k == 0), stop=(k == KD - 1)))
                        for k in range(KD):
                            m_mm = OP("pe", sPE if k == KD - 1 else None,
                                      lambda e, k=k, j=j, dt2=dt2, b=bb: e.matmul(
                                          pb[b + 1][:, 0:256],
                                          ff_w2[j][:, k, dt2 * 128:(dt2 + 1) * 128],
                                          ff_xne[:, k, :], start=(k == 0), stop=(k == KD - 1)))
                        W("se", m_mm)
                        rsl.wait(W, "se", dff % 2)
                        m_sl = OP("se", sSE, lambda e, e_=e_, dff=dff, b=bb: e.activation(
                            ff_sl[dff % 2][:, :], pb[b][:, 0:256], AF.Silu,
                            bias=b1c[:, e_, dff:dff + 1], scale=1.0))
                        W("ve", m_mm)
                        W("ve", m_sl)
                        m_h = OP("ve", sVE, lambda e, e_=e_, dff=dff, b=bb: e.scalar_tensor_tensor(
                            ff_h[:, dff, :], pb[b + 1][:, 0:256], b2c[:, e_, dff:dff + 1],
                            ff_sl[dff % 2][:, :], ALU.add, ALU.mult))
                        DR("ve")
                        h_marks.append(m_h)
                        rsl.set(dff % 2, m_h)
                        if dt2 == 1:
                            rw12.set(j, m_mm)
                W("pe", h_marks[-1])
                W("pe", h_marks[-2])
                bank_wait("pe", 4, 5, 6, 7)
                for gg in range(16):
                    j = gg % 2
                    rw3.wait(W, "sy", j)
                    m_wl = OP("sy", sW3[j], lambda e, e_=e_, gg=gg, j=j: e.dma_start(
                        out=ff_w3[j][:, :, :],
                        in_=w3s[e_, gg * 256:(gg + 1) * 256, :].rearrange(
                            "(k p) m -> p k m", p=128)), 16)
                    W("pe", m_wl)
                    W("pe", h_marks[gg * 2 + 1])
                    for jj in range(2):
                        for jh in range(2):
                            for dh in range(2):
                                bk = 4 + 2 * jh + dh
                                m_w3 = OP("pe",
                                          sPE if (jj == 1 and jh == 1 and dh == 1) else None,
                                          lambda e, gg=gg, j=j, jj=jj, jh=jh, dh=dh, bk=bk:
                                          e.matmul(
                                              pb[bk][:, :],
                                              ff_h[:, gg * 2 + jj, jh * 128:(jh + 1) * 128],
                                              ff_w3[j][:, jj, dh * 512:(dh + 1) * 512],
                                              start=(gg == 0 and jj == 0),
                                              stop=(gg == 15 and jj == 1)))
                    rw3.set(j, m_w3)
                for jh in range(2):
                    for dh in range(2):
                        bk = 4 + 2 * jh + dh
                        W("se", m_w3)
                        m_ye = OP("se", sSE, lambda e, jh=jh, dh=dh, bk=bk: e.activation(
                            ff_ywe[:, jh, dh, :], pb[bk][:, :], AF.Copy,
                            bias=0.0, scale=1.0))
                        bankf[bk] = m_ye
                m_ye_last = m_ye
                W("pe", m_PT)
                W("pe", m_ye_last)
                for tc in range(4):
                    for dh in range(2):
                        bk = (tc * 2 + dh) % 2
                        bank_wait("pe", bk)
                        for jh in range(2):
                            m_sc = OP("pe", sPE if jh == 1 else None,
                                      lambda e, tc=tc, dh=dh, jh=jh, bk=bk: e.matmul(
                                          pb[bk][:, :], ff_PT[:, jh, tc * 128:(tc + 1) * 128],
                                          ff_ywe[:, jh, dh, :],
                                          start=(jh == 0), stop=(jh == 1)))
                        W("ve", m_sc)
                        m_ya = OP("ve", sVE,
                                  lambda e, e_=e_, tc=tc, dh=dh, bk=bk: e.scalar_tensor_tensor(
                                      ff_ytok[:, tc, dh * 512:(dh + 1) * 512], pb[bk][:, :],
                                      ff_cwt[:, tc, e_:e_ + 1],
                                      ff_ytok[:, tc, dh * 512:(dh + 1) * 512],
                                      ALU.mult, ALU.add))
                        DR("ve")
                        bankf[bk] = m_ya
                m_scat_pe = m_sc
                m_ya_last = m_ya
            for kd in range(KD):
                bk = 2 + kd % 2
                W("pe", m_ya_last)
                bank_wait("pe", bk)
                for tc in range(4):
                    m_t = OP("pe", sPE if tc == 3 else None,
                             lambda e, kd=kd, tc=tc, bk=bk: e.transpose(
                                 pb[bk][:, tc * 128:(tc + 1) * 128],
                                 ff_ytok[:, tc, kd * 128:(kd + 1) * 128], ident[:, :]))
                W("ve", m_t)
                m_vadd = OP("ve", sVE, lambda e, kd=kd, c0=c0, bk=bk: e.tensor_tensor(
                    xT[:, kd, c0:c0 + TC], pb[bk][:, :], xT[:, kd, c0:c0 + TC], ALU.add))
                DR("ve")
                bankf[bk] = m_vadd
            m_last_vadd = m_vadd
        return m_last_vadd

    if stage in ("ffn1", "attn_b", "full"):
        m_res = emit_ffn(1, m_res)

    if stage in ("attn_b", "full"):
        for m_ in m_conv_all:
            W("sy", m_)
        W("sy", m_res)
        m_selb = OP("sy", sLD, lambda e: e.dma_start(out=ab_selb[:, :], in_=selb_h[:, :]), 16)
        OP("sy", sLD, lambda e: e.dma_start(
            out=ab_wxT[:, :, :], in_=wxTs.ap().rearrange("(k p) m -> p k m", p=128)), 16)
        m_wx = OP("sy", sLD, lambda e: e.dma_start(out=ab_wdtT[:, :], in_=wdtTs[:, :]), 16)
        m_wdt = m_wx
        m_selb = m_wx

        g_exp = {}
        g_scan = {}
        g_scan_it = {}
        g_yadd = {}
        g_dbx = {}
        g_hc = {}
        git = [0]
        rwing = Rot(2)
        rwog = Rot(2)
        for j in (0, 1):
            rwing.last[j] = m_res
            rwog.last[j] = m_res
        scan_rel = [None, None]
        m_res_ve = m_res
        for c in range(NCH):
            c0 = c * TC
            m_xn, _, _, _ = emit_ln(c, ab_xn, ab_ln_sq, ab_ln_rows,
                                    pw_pe=[m_res], pw_se=[m_res, m_res_ve],
                                    pw_ve=[m_conv_all[0], m_conv_all[1], m_res_ve])
            xz_m = {}
            for g in range(16):
                j = g % 2
                rwing.wait(W, "sy", j)
                if g == 0:
                    W("sy", scan_rel[0])
                    W("sy", scan_rel[1])
                if c == 0 and g == 0:
                    for m_ in conv_done[("win",)]:
                        W("sy", m_)
                m_l = OP("sy", sWing[j], lambda e, g=g, j=j: e.dma_start(
                    out=ab_wing[j][:, :, :],
                    in_=winTs[:, g * 256:(g + 1) * 256].rearrange("(k p) m -> p k m", p=128)), 16)
                for m2 in range(2):
                    mi = g * 2 + m2
                    bb = mi % 2
                    W("pe", m_l)
                    W("pe", m_xn)
                    bank_wait("pe", bb)
                    for k in range(KD):
                        m_mm = OP("pe", sPE if k == KD - 1 else None,
                                  lambda e, k=k, j=j, m2=m2, b=bb: e.matmul(
                                      pb[b][:, :], ab_wing[j][:, k, m2 * 128:(m2 + 1) * 128],
                                      ab_xn[:, k, :], start=(k == 0), stop=(k == KD - 1)))
                    if m2 == 1:
                        rwing.set(j, m_mm)
                    if mi < KI:
                        W("ve", m_mm)
                        if c == 0 and mi == 0:
                            W("ve", (sGP, m_tailz[1]))
                        xr = ab_xiraw[mi % 2]
                        OP("ve", None, lambda e, mi=mi, xr=xr: e.tensor_copy(
                            xr[:, 0:3], tail[:, mi, 0:3]))
                        DR("ve")
                        m_xr = OP("ve", sVE, lambda e, xr=xr, b=bb: e.tensor_copy(
                            xr[:, 3:515], pb[b][:, :]))
                        bankf[bb] = m_xr
                        DR("ve")
                        OP("ve", None, lambda e, mi=mi, xr=xr: e.tensor_copy(
                            tail[:, mi, 0:3], xr[:, 512:515]))
                        DR("ve")
                        cv = ab_cv[mi % 2]
                        if mi >= 2:
                            W("ve", xz_m[mi - 2])
                        OP("ve", None, lambda e, mi=mi, xr=xr, cv=cv: e.tensor_scalar(
                            cv[:, :], xr[:, 0:512], convw[:, mi, 0:1], None, ALU.mult))
                        DR("ve")
                        for jj in range(1, 3):
                            OP("ve", None, lambda e, mi=mi, xr=xr, cv=cv, jj=jj:
                               e.scalar_tensor_tensor(cv[:, :], xr[:, jj:jj + 512],
                                                      convw[:, mi, jj:jj + 1], cv[:, :],
                                                      ALU.mult, ALU.add))
                            DR("ve")
                        m_cv = OP("ve", sVE, lambda e, mi=mi, xr=xr, cv=cv:
                                  e.scalar_tensor_tensor(cv[:, :], xr[:, 3:515][:, 0:512],
                                                         convw[:, mi, 3:4], cv[:, :],
                                                         ALU.mult, ALU.add))
                        DR("ve")
                        W("se", m_cv)
                        m_xi = OP("se", sSE, lambda e, mi=mi, cv=cv: e.activation(
                            ab_xi[:, mi, :], cv[:, :], AF.Silu,
                            bias=convb[:, mi:mi + 1], scale=1.0))
                        xz_m[mi] = m_xi
                    else:
                        W("se", m_mm)
                        m_z = OP("se", sSE, lambda e, mi=mi, b=bb: e.activation(
                            ab_zs[:, mi - KI, :], pb[b][:, :], AF.Silu, bias=0.0, scale=1.0))
                        bankf[bb] = m_z
                        xz_m[mi] = m_z
            m_z_all = xz_m[2 * KI - 1]

            if c == 0:
                W("pe", m_wx)
            bank_wait("pe", 2, 3, 4)
            for mi in range(KI):
                W("pe", xz_m[mi])
                OP("pe", None, lambda e, mi=mi: e.matmul(
                    pb[2][0:DTR, :], ab_wxT[:, mi, 0:DTR], ab_xi[:, mi, :],
                    start=(mi == 0), stop=(mi == KI - 1)))
                OP("pe", None, lambda e, mi=mi: e.matmul(
                    pb[3][0:DST, :], ab_wxT[:, mi, DTR:DTR + DST], ab_xi[:, mi, :],
                    start=(mi == 0), stop=(mi == KI - 1)))
                m_xd = OP("pe", sPE if mi == KI - 1 else None,
                          lambda e, mi=mi: e.matmul(
                              pb[4][0:DST, :], ab_wxT[:, mi, DTR + DST:DTR + 2 * DST],
                              ab_xi[:, mi, :], start=(mi == 0), stop=(mi == KI - 1)))
            W("ve", m_xd)
            m_dtr = OP("ve", sVE, lambda e: e.tensor_copy(ab_dtr[:, :], pb[2][0:DTR, :]))
            DR("ve")
            bankf[2] = m_dtr
            W("se", m_xd)
            OP("se", None, lambda e: e.copy(abBc[:, :], pb[3][0:DST, :]))
            DR("se")
            m_BC = OP("se", sSE, lambda e: e.copy(abCc[:, :], pb[4][0:DST, :]))
            bankf[3] = m_BC
            bankf[4] = m_BC

            if c == 0:
                W("pe", m_wdt)
            W("pe", m_dtr)
            dt_m = []
            for mi in range(KI):
                bb = 5 + mi % 2
                bank_wait("pe", bb)
                m_dp = OP("pe", sPE, lambda e, mi=mi, b=bb: e.matmul(
                    pb[b][:, :], ab_wdtT[:, mi * 128:(mi + 1) * 128], ab_dtr[:, :],
                    start=True, stop=True))
                W("ve", m_dp)
                v = ab_spl[mi % 2]
                av = ab_cv[mi % 2]
                m_v0 = OP("ve", sVE, lambda e, mi=mi, v=v, b=bb: e.tensor_scalar(
                    v[:, :], pb[b][:, :], bdt[:, mi:mi + 1], None, ALU.add))
                bankf[bb] = m_v0
                DR("ve")
                OP("ve", None, lambda e, v=v, av=av: e.tensor_scalar(
                    av[:, :], v[:, :], -1.0, None, ALU.mult))
                DR("ve")
                m_av = OP("ve", sVE, lambda e, v=v, av=av: e.tensor_tensor(
                    av[:, :], v[:, :], av[:, :], ALU.max))
                W("se", m_av)
                OP("se", None, lambda e, av=av: e.activation(av[:, :], av[:, :], AF.Exp,
                                                             bias=0.0, scale=-1.0))
                DR("se")
                m_l1p = OP("se", sSE, lambda e, av=av: e.activation(av[:, :], av[:, :], AF.Ln,
                                                                   bias=1.0, scale=1.0))
                DR("ve")
                OP("ve", None, lambda e, v=v: e.tensor_scalar_max(v[:, :], v[:, :], 0.0))
                DR("ve")
                W("ve", m_l1p)
                m_dt = OP("ve", sVE, lambda e, mi=mi, v=v, av=av: e.tensor_tensor(
                    ab_dt[:, mi, :], v[:, :], av[:, :], ALU.add))
                DR("ve")
                dt_m.append(m_dt)
            m_dt_all = dt_m[-1]

            m_u = None
            W("ve", (sGP, sGP.n))
            for mi in range(KI):
                W("ve", dt_m[mi])
                OP("ve", None, lambda e, mi=mi: e.tensor_scalar(
                    ab_y[:, mi, :], ab_xi[:, mi, :], dskip[:, mi:mi + 1], None, ALU.mult))
                DR("ve")
                m_u = OP("ve", sVE, lambda e, mi=mi: e.tensor_tensor(
                    ab_xi[:, mi, :], ab_dt[:, mi, :], ab_xi[:, mi, :], ALU.mult))
                DR("ve")
            m_u_all = m_u
            ab_u = ab_xi

            if c == 0:
                W("ve", (sGP, m_carryz[1]))
                W("ve", m_aneg)
                W("se", m_aneg)
            W("se", m_dt_all)

            def emit_tail(itp):
                mip = itp % KI
                jbp = itp % 2
                g_hc[itp] = OP("ve", sVE, lambda e, jbp=jbp: e.tensor_tensor(
                    ab_dBx[jbp][:, :], ab_h[jbp][:, :], ab_Cb[:, :], ALU.mult))
                DR("ve")
                g_yadd[itp] = OP("ve", sVE, lambda e, mip=mip, jbp=jbp: e.tensor_tensor(
                    ab_y[:, mip, :], ab_y[:, mip, :], ab_dBx[jbp][:, :], ALU.add))
                DR("ve")

            for n in range(DST):
                it0 = git[0] * KI
                W("pe", m_BC)
                if c == 0 and n == 0:
                    W("pe", m_selb)
                bank_wait("pe", 7, 2)
                m_bbm = OP("pe", sPE, lambda e, n=n: e.matmul(
                    pb[7][:, :], ab_selb[:, n * 128:(n + 1) * 128], abBc[:, :],
                    start=True, stop=True))
                m_cbm = OP("pe", sPE, lambda e, n=n: e.matmul(
                    pb[2][:, :], ab_selb[:, n * 128:(n + 1) * 128], abCc[:, :],
                    start=True, stop=True))
                W("se", m_cbm)
                if it0 >= 1:
                    W("se", g_hc[it0 - 1])
                OP("se", None, lambda e: e.copy(ab_Bb[:, :], pb[7][:, :]))
                DR("se")
                m_bcb = OP("se", sSE, lambda e: e.copy(ab_Cb[:, :], pb[2][:, :]))
                bankf[7] = m_bcb
                bankf[2] = m_bcb
                W("gp", m_bcb)
                if n == 0:
                    W("gp", m_u_all)
                for mi0 in range(2):
                    itb = it0 + mi0
                    if itb >= 2:
                        W("gp", g_yadd[itb - 2])
                    g_dbx[itb] = OP("gp", sGP, lambda e, mi0=mi0, jb=itb % 2, itb=itb: e.tensor_tensor(
                        ab_dBx[jb][:, :], ab_u[:, (itb % KI), :], ab_Bb[:, :], ALU.mult))
                for mi in range(KI):
                    it = it0 + mi
                    jb = it % 2
                    if it >= 1:
                        W("se", g_scan_it[it - 1])
                        OP("se", None, lambda e, it=it: e.copy(
                            carry[:, (it - 1) % KI, (it - 1) // KI % DST:
                                  (it - 1) // KI % DST + 1],
                            ab_h[(it - 1) % 2][:, 511:512]))
                        DR("se")
                    g_exp[it] = OP("se", sSE, lambda e, mi=mi, n=n: e.activation(
                        ab_dA[0][:, :], ab_dt[:, mi, :], AF.Exp, bias=0.0,
                        scale=A_sb[:, mi, n:n + 1]))
                    if 1 <= mi <= KI - 2:
                        W("gp", g_yadd[it - 1])
                        g_dbx[it + 1] = OP("gp", sGP, lambda e, mi=mi, jb=(it + 1) % 2:
                                           e.tensor_tensor(ab_dBx[jb][:, :],
                                                           ab_u[:, mi + 1, :],
                                                           ab_Bb[:, :], ALU.mult))
                    if it == 0 and c == 0:
                        W("ve", m_u_all)
                    W("ve", g_exp[it])
                    W("ve", g_dbx[it])
                    m_sc = OP("ve", sVE, lambda e, mi=mi, n=n, jb=jb: e.tensor_tensor_scan(
                        ab_h[jb][:, :], ab_dA[0][:, :], ab_dBx[jb][:, :],
                        carry[:, mi, n:n + 1], ALU.mult, ALU.add))
                    g_scan_it[it] = m_sc
                    DR("ve")
                    emit_tail(it)
                git[0] += 1
            m_y = g_yadd[git[0] * KI - 1]
            itf = git[0] * KI - 1
            W("se", g_scan_it[itf])
            m_carryf = OP("se", sSE, lambda e, itf=itf: e.copy(
                carry[:, itf % KI, (itf // KI) % DST:(itf // KI) % DST + 1],
                ab_h[itf % 2][:, 511:512]))
            scan_rel = [m_carryf, m_y]

            m_yz = None
            W("ve", m_z_all)
            for mi in range(KI):
                m_yz = OP("ve", sVE, lambda e, mi=mi: e.tensor_tensor(
                    ab_yz[:, mi, :], ab_y[:, mi, :], ab_zs[:, mi, :], ALU.mult))
                DR("ve")
            for g2 in range(4):
                j = g2 % 2
                rwog.wait(W, "sy", j)
                W("sy", m_yz)
                if c == 0 and g2 == 0:
                    for m_ in conv_done[("wout",)]:
                        W("sy", m_)
                m_wl = OP("sy", sWog[j], lambda e, g2=g2, j=j: e.dma_start(
                    out=ab_wog[j][:, :, :],
                    in_=woutTs[:, g2 * 256:(g2 + 1) * 256].rearrange(
                        "(k p) m -> p k m", p=128)), 16)
                for m2 in range(2):
                    md = g2 * 2 + m2
                    bb = 3 + md % 2
                    W("pe", m_wl)
                    W("pe", m_yz)
                    bank_wait("pe", bb)
                    for kk in range(KI):
                        m_mm = OP("pe", sPE if kk == KI - 1 else None,
                                  lambda e, kk=kk, j=j, m2=m2, b=bb: e.matmul(
                                      pb[b][:, :], ab_wog[j][:, kk, m2 * 128:(m2 + 1) * 128],
                                      ab_yz[:, kk, :], start=(kk == 0), stop=(kk == KI - 1)))
                    if m2 == 1:
                        rwog.set(j, m_mm)
                    W("ve", m_mm)
                    m_vadd = OP("ve", sVE, lambda e, md=md, c0=c0, b=bb: e.tensor_tensor(
                        xT[:, md, c0:c0 + TC], pb[b][:, :], xT[:, md, c0:c0 + TC], ALU.add))
                    DR("ve")
                    bankf[bb] = m_vadd
            m_res_ve = m_vadd
        m_res = m_res_ve

    if stage == "full":
        m_res = emit_ffn(2, m_res)

    W("sy", (sVE, sVE.n))
    W("sy", (sSE, sSE.n))
    W("sy", (sPE, sPE.n))
    W("sy", (sGP, sGP.n))
    for s_ in [sLD, sW12[0], sW12[1], sW3[0], sW3[1], sWing[0], sWing[1],
               sWog[0], sWog[1], sCVc, sCVs[0], sCVs[1]]:
        if s_.n:
            W("sy", (s_, s_.n))
    OP("sy", sOUT, lambda e: e.dma_start(
        out=outT_h.ap().rearrange("(k p) t -> p k t", p=128), in_=xT[:, :, :]), 16)
    W("sy", (sOUT, 16))

    with nc.Block() as block:
        block.sync(lambda e: [f(e) for f in qq["sy"]])
        block.tensor(lambda e: [f(e) for f in qq["pe"]])
        block.vector(lambda e: [f(e) for f in qq["ve"]])
        block.scalar(lambda e: [f(e) for f in qq["se"]])
        block.gpsimd(lambda e: [f(e) for f in qq["gp"]])
    return nc


def _prep_shared(i):
    f32 = np.float32
    ffc = np.zeros((128, 260), f32)
    ffc[:, 0:256] = np.arange(256, dtype=f32)[None, :]
    ffc[:, 256] = np.arange(128, dtype=f32)
    ffc[:, 257] = 256.0 * np.arange(128, dtype=f32)
    selb = np.zeros((DST, DST * 128), f32)
    for n in range(DST):
        selb[n, n * 128:(n + 1) * 128] = 1.0
    sel2 = np.zeros((2, 128), f32)
    sel2[0, 0:64] = 1.0
    sel2[1, 64:128] = 1.0
    blk2 = np.zeros((128, 2), f32)
    blk2[0:64, 0] = 1.0
    blk2[64:128, 1] = 1.0
    c = np.ascontiguousarray
    return {
        "wq": c(np.asarray(i["L0_attn_a_Wq"], f32)),
        "wkv": c(np.asarray(i["L0_attn_a_Wkv"], f32)),
        "wo": c(np.asarray(i["L0_attn_a_Wo"], f32)),
        "wr": c(np.asarray(i["L0_ffn_Wr"], f32)),
        "brc": c(np.asarray(i["L0_ffn_br"], f32).reshape(NE, 1)),
        "w1": c(np.asarray(i["L0_ffn_W1"], f32)),
        "w2": c(np.asarray(i["L0_ffn_W2"], f32)),
        "w3": c(np.asarray(i["L0_ffn_W3"], f32)),
        "b1": c(np.asarray(i["L0_ffn_b1"], f32).reshape(NE, NDFF, 128).transpose(2, 0, 1)),
        "b2": c(np.asarray(i["L0_ffn_b2"], f32).reshape(NE, NDFF, 128).transpose(2, 0, 1)),
        "b3r": c(np.asarray(i["L0_ffn_b3"], f32)),
        "ffc": ffc,
        "identbc": np.eye(128, dtype=ml_dtypes.bfloat16),
        "winT": c(np.asarray(i["L0_attn_b_W_in"], f32).T),
        "wxT": c(np.asarray(i["L0_attn_b_W_x"], f32).T),
        "wdtT": c(np.asarray(i["L0_attn_b_W_dt"], f32).T),
        "woutT": c(np.asarray(i["L0_attn_b_W_out"], f32).T),
        "convw": c(np.asarray(i["L0_attn_b_conv_w"], f32)[:, 0, :]
                   .reshape(KI, 128, 4).transpose(1, 0, 2)),
        "convb": c(np.asarray(i["L0_attn_b_conv_b"], f32).reshape(KI, 128).T),
        "bdt": c(np.asarray(i["L0_attn_b_b_dt"], f32).reshape(KI, 128).T),
        "alog": c(np.asarray(i["L0_attn_b_A_log"], f32).reshape(KI, 128, DST)
                  .transpose(1, 0, 2)),
        "dskip": c(np.asarray(i["L0_attn_b_D_skip"], f32).reshape(KI, 128).T),
        "identc": np.eye(128, dtype=f32),
        "onesc": np.ones((128, 128), f32),
        "sel2c": sel2,
        "blk2c": blk2,
        "selbc": selb,
    }


def kernel(**inputs):
    x = np.asarray(inputs["L0_attn_a_x"], np.float32)
    B = x.shape[0]
    shared = _prep_shared(inputs)
    nc = build("full")
    in_maps = []
    for b in range(B):
        m = dict(shared)
        m["xT"] = np.ascontiguousarray(x[b].T)
        in_maps.append(m)
    res = run_bass_kernel_spmd(nc, in_maps, core_ids=list(range(B)))
    out = np.stack([np.asarray(r["outT"]).T for r in res.results], axis=0)
    return out.astype(np.float32)
